# version 04
# baseline (speedup 1.0000x reference)
import numpy as np
import concourse.bass as bass
import concourse.mybir as mybir
from concourse.bass_utils import run_bass_kernel_spmd
from contextlib import ExitStack

F32 = mybir.dt.float32
BF16 = mybir.dt.bfloat16
AF = mybir.ActivationFunctionType
ALU = mybir.AluOpType
AX = mybir.AxisListType

P = 128
D = 1024
NCH = 8
T = 4096
BC = 2
NTOK = BC * T
NTILE = NTOK // P
TCB = 256
NB_B = T // TCB
BLK = 1024
NBLK = NTOK // BLK
FF = 4096
NE = 8
CAP = 10240.0
EPS = 1e-5

DEBUG = False
DEBUG_B = False
DEBUG_B3 = False
IT_DUMP = 3
B_SERIAL = False
PHASES = "ABCD"


class Sem:

    _DRAIN_ENGINES = ("DVE", "Activation", "Pool", "PE")

    def __init__(self, nc, stack, name):
        self.nc = nc
        self.h = stack.enter_context(nc.semaphore(name))
        self.n = 0

    def inc(self, instr, k=1):
        ename = str(getattr(instr.ins, "engine", ""))
        iname = type(instr.ins).__name__
        if any(e in ename for e in self._DRAIN_ENGINES) and "Collective" not in iname:
            eng = {"DVE": self.nc.vector, "Activation": self.nc.scalar,
                   "Pool": self.nc.gpsimd, "PE": self.nc.tensor}[
                       next(e for e in self._DRAIN_ENGINES if e in ename)]
            eng.drain().then_inc(self.h, k)
        else:
            instr.then_inc(self.h, k)
        self.n += k
        return self.n

    def dinc(self, eng, k=1):
        eng.drain().then_inc(self.h, k)
        self.n += k
        return self.n

    def wait(self, eng, tok=None):
        eng.wait_ge(self.h, self.n if tok is None else tok)


def build_nc():
    nc = bass.Bass("TRN2", target_bir_lowering=False)

    x_in = nc.dram_tensor("x", [BC, T, D], F32, kind="ExternalInput")
    wkT = nc.dram_tensor("wkT", [D, D], F32, kind="ExternalInput")
    wvT = nc.dram_tensor("wvT", [D, D], F32, kind="ExternalInput")
    wrT = nc.dram_tensor("wrT", [D, D], F32, kind="ExternalInput")
    woT = nc.dram_tensor("woT", [D, D], F32, kind="ExternalInput")
    wrout = nc.dram_tensor("wrout", [D, NE], F32, kind="ExternalInput")
    w1 = nc.dram_tensor("w1", [NE, D, FF], F32, kind="ExternalInput")
    w2 = nc.dram_tensor("w2", [NE, FF, D], F32, kind="ExternalInput")
    b1T = nc.dram_tensor("b1T", [NE, P, FF // P], F32, kind="ExternalInput")
    b2 = nc.dram_tensor("b2", [NE, D], F32, kind="ExternalInput")
    tdT = nc.dram_tensor("tdT", [P, NCH], F32, kind="ExternalInput")
    tfT = nc.dram_tensor("tfT", [P, NCH], F32, kind="ExternalInput")
    tmkT = nc.dram_tensor("tmkT", [P, NCH], F32, kind="ExternalInput")
    tmvT = nc.dram_tensor("tmvT", [P, NCH], F32, kind="ExternalInput")
    tmrT = nc.dram_tensor("tmrT", [P, NCH], F32, kind="ExternalInput")
    ident = nc.dram_tensor("ident", [P, P], F32, kind="ExternalInput")
    triu = nc.dram_tensor("triu", [P, P], F32, kind="ExternalInput")
    onesr = nc.dram_tensor("onesr", [1, P], F32, kind="ExternalInput")
    prefc = nc.dram_tensor("prefc", [NE, 1], F32, kind="ExternalInput")
    out = nc.dram_tensor("out", [BC, T, D], F32, kind="ExternalOutput")

    xnsc = nc.dram_tensor("xnsc", [BC, NCH, P, T], BF16)
    ssc = nc.dram_tensor("ssc", [BC, NCH, P, T], F32)
    moesc = nc.dram_tensor("moesc", [NBLK, NCH, P, BLK], F32)
    w1sc = nc.dram_tensor("w1sc", [NE, D, FF], BF16)
    w2sc = nc.dram_tensor("w2sc", [NE, FF, D], BF16)
    wt8sc = nc.dram_tensor("wt8sc", [NE, NTOK], F32)
    ccin = nc.dram_tensor("ccin", [1, 16], F32)
    ccout = nc.dram_tensor("ccout", [NE, 16], F32)

    dbg = {}
    if DEBUG_B3:
        dbg["rwall"] = nc.dram_tensor("dbg_rwall", [BC * NB_B, P, NCH, TCB], BF16, kind="ExternalOutput")
        dbg["wkvall"] = nc.dram_tensor("dbg_wkvall", [BC * NB_B, P, NCH, TCB], F32, kind="ExternalOutput")
        dbg["sraall"] = nc.dram_tensor("dbg_sraall", [BC * NB_B, P, NCH, TCB], F32, kind="ExternalOutput")
        dbg["ekvall"] = nc.dram_tensor("dbg_ekvall", [BC * NB_B, P, NCH, TCB + 1], F32, kind="ExternalOutput")
        dbg["ekall"] = nc.dram_tensor("dbg_ekall", [BC * NB_B, P, NCH, TCB + 1], F32, kind="ExternalOutput")
        dbg["aexall"] = nc.dram_tensor("dbg_aexall", [BC * NB_B, P, NCH, TCB], F32, kind="ExternalOutput")
        dbg["bexall"] = nc.dram_tensor("dbg_bexall", [BC * NB_B, P, NCH, TCB], F32, kind="ExternalOutput")
    if DEBUG_B:
        for nm, dt_ in [("mk", BF16), ("ek", F32), ("ekv", F32), ("aex", F32),
                        ("bex", F32), ("den", F32), ("sra", F32), ("rw", BF16),
                        ("sev", F32), ("carA", F32), ("carB", F32)]:
            shape = [P, NCH] if nm in ("carA", "carB") else ([P, NCH, TCB + 1] if nm in ("ek", "ekv") else [P, NCH, TCB])
            dbg[nm] = nc.dram_tensor("dbgB_" + nm, shape, dt_, kind="ExternalOutput")
    if DEBUG:
        dbg["xn"] = nc.dram_tensor("dbg_xn", [BC, NCH, P, T], BF16, kind="ExternalOutput")
        dbg["s"] = nc.dram_tensor("dbg_s", [BC, NCH, P, T], F32, kind="ExternalOutput")
        dbg["s2"] = nc.dram_tensor("dbg_s2", [BC, NCH, P, T], F32, kind="ExternalOutput")
        dbg["moe"] = nc.dram_tensor("dbg_moe", [NBLK, NCH, P, BLK], F32, kind="ExternalOutput")
        dbg["wt8"] = nc.dram_tensor("dbg_wt8", [NE, NTOK], F32, kind="ExternalOutput")

    with ExitStack() as top:
        def sbuf(st, name, shape, dtype):
            return st.enter_context(nc.sbuf_tensor(name, shape, dtype))

        wrout_s = sbuf(top, "wrout_s", [P, NCH, NE], F32)
        b1_s = sbuf(top, "b1_s", [P, NE, FF // P], F32)
        b2_s = sbuf(top, "b2_s", [NE, D], F32)
        td_s = sbuf(top, "td_s", [P, NCH], F32)
        ew_s = sbuf(top, "ew_s", [P, NCH], F32)
        eu_s = sbuf(top, "eu_s", [P, NCH], F32)
        tmk_s = sbuf(top, "tmk_s", [P, NCH], F32)
        tmv_s = sbuf(top, "tmv_s", [P, NCH], F32)
        tmr_s = sbuf(top, "tmr_s", [P, NCH], F32)
        tmk1_s = sbuf(top, "tmk1_s", [P, NCH], F32)
        tmv1_s = sbuf(top, "tmv1_s", [P, NCH], F32)
        tmr1_s = sbuf(top, "tmr1_s", [P, NCH], F32)
        id_s = sbuf(top, "id_s", [P, P], F32)
        triu_s = sbuf(top, "triu_s", [P, P], F32)
        ones_s = sbuf(top, "ones_s", [1, P], F32)
        pref_s = sbuf(top, "pref_s", [NE, 1], F32)
        coff_s = sbuf(top, "coff_s", [1, 16], F32)

        pb = [top.enter_context(nc.psum_tensor(f"pb{i}", [P, 512], F32)) for i in range(8)]

        S = lambda name: Sem(nc, top, name)
        sConst = S("sConst")
        sPrep = S("sPrep")
        sXL = [S("sXL0"), S("sXL1")]
        sVa = S("sVa")
        sSa = S("sSa")
        sTa = S("sTa")
        sGa = S("sGa")
        sXnW = [S("sXnW0"), S("sXnW1")]
        sCC = S("sCC")
        sCCC = S("sCCC")
        sVb = S("sVb")
        sSb = S("sSb")
        sTb = S("sTb")
        sXW = [S("sXW0"), S("sXW1")]
        sSw = [S("sSw0"), S("sSw1")]
        sPrepR = S("sPrepR")
        sWT8 = S("sWT8")
        sXnB = S("sXnB")
        sWt8b = S("sWt8b")
        sWrow = S("sWrow")
        sW1c = [S("sW1c0"), S("sW1c1")]
        sW2c = [S("sW2c0"), S("sW2c1")]
        sTc = S("sTc")
        sGc = S("sGc")
        sVc = S("sVc")
        sMw = S("sMw")
        sDbgB = S("sDbgB")
        sLd = S("sLd")
        sVd = S("sVd")
        sSd = S("sSd")
        sTd = S("sTd")
        sOut = [S("sOut0"), S("sOut1")]

        blk_guard = top.enter_context(nc.Block())

        @blk_guard.gpsimd
        def _(g):
            sy = nc.sync
            v = nc.vector
            sc = nc.scalar
            t = nc.tensor

            for dst, src in [
                (wrout_s[:, :, :], wrout.ap().rearrange("(kc p) e -> p kc e", p=P)),
                (b1_s[:, :, :], b1T.ap().rearrange("e p m -> p e m")),
                (b2_s[:, :], b2.ap()),
                (td_s[:, :], tdT.ap()),
                (eu_s[:, :], tfT.ap()),
                (tmk_s[:, :], tmkT.ap()),
                (tmv_s[:, :], tmvT.ap()),
                (tmr_s[:, :], tmrT.ap()),
                (id_s[:, :], ident.ap()),
                (triu_s[:, :], triu.ap()),
                (ones_s[:, :], onesr.ap()),
                (pref_s[:, :], prefc.ap()),
            ]:
                sy.dma_start(out=dst, in_=src).then_inc(sConst.h, 16)
                sConst.n += 16
            tok_const = sConst.n

            prep_tok_e = []
            with ExitStack() as ab:
                wk_b = sbuf(ab, "wk_b", [P, NCH, D], BF16)
                wv_b = sbuf(ab, "wv_b", [P, NCH, D], BF16)
                wr_b = sbuf(ab, "wr_b", [P, NCH, D], BF16)
                wo_b = sbuf(ab, "wo_b", [P, NCH, D], BF16)
                for dst, src in [(wk_b, wkT), (wv_b, wvT), (wr_b, wrT), (wo_b, woT)]:
                    g.dma_start(out=dst[:, :, :],
                                in_=src.ap().rearrange("(kc p) m -> p kc m", p=P)).then_inc(sPrepR.h, 16)
                    sPrepR.n += 16
                tok_rwkv_w = sPrepR.n
                for e in range(NE):
                    g.dma_start(out=w1sc.ap()[e], in_=w1.ap()[e]).then_inc(sPrep.h, 16)
                    sPrep.n += 16
                    g.dma_start(out=w2sc.ap()[e], in_=w2.ap()[e]).then_inc(sPrep.h, 16)
                    sPrep.n += 16
                    prep_tok_e.append(sPrep.n)

                sConst.wait(v, tok_const)
                v.tensor_scalar(tmk1_s[:, :], tmk_s[:, :], -1.0, 1.0, ALU.mult, ALU.add)
                v.tensor_scalar(tmv1_s[:, :], tmv_s[:, :], -1.0, 1.0, ALU.mult, ALU.add)
                tok_tm1 = sVa.inc(v.tensor_scalar(tmr1_s[:, :], tmr_s[:, :], -1.0, 1.0, ALU.mult, ALU.add))
                sConst.wait(sc, tok_const)
                sc.activation(ew_s[:, :], td_s[:, :], AF.Exp)
                sc.activation(ew_s[:, :], ew_s[:, :], AF.Exp, scale=-1.0)
                tok_ewu = sSa.inc(sc.activation(eu_s[:, :], eu_s[:, :], AF.Exp))

                asc = ExitStack()
                topv_s = sbuf(asc, "topv_s", [P, NTILE, 2], F32)
                mask_s = sbuf(asc, "mask_s", [P, NTILE, 16], F32)
                off_s = sbuf(asc, "off_s", [1, NTILE, 16], F32)
                run_s = sbuf(asc, "run_s", [1, 16], F32)
                tot_s = sbuf(asc, "tot_s", [1, 16], F32)
                agb_s = sbuf(asc, "agb_s", [NE, 16], F32)
                xt = [sbuf(asc, "xt0", [P, D], F32), sbuf(asc, "xt1", [P, D], F32)]
                sqd = sbuf(asc, "sqd", [P, D], F32)
                xn_f = [sbuf(asc, "xnf0", [P, D], F32), sbuf(asc, "xnf1", [P, D], F32)]
                xnT_f = [sbuf(asc, "xnTf0", [P, NCH, P], F32), sbuf(asc, "xnTf1", [P, NCH, P], F32)]
                xnT_b = [sbuf(asc, "xnTb0", [P, NCH, P], BF16), sbuf(asc, "xnTb1", [P, NCH, P], BF16)]
                ssum = [sbuf(asc, "ssum0", [P, 1], F32), sbuf(asc, "ssum1", [P, 1], F32)]
                ssq = [sbuf(asc, "ssq0", [P, 1], F32), sbuf(asc, "ssq1", [P, 1], F32)]
                smean = [sbuf(asc, "smean0", [P, 1], F32), sbuf(asc, "smean1", [P, 1], F32)]
                svar = [sbuf(asc, "svar0", [P, 1], F32), sbuf(asc, "svar1", [P, 1], F32)]
                srstd = [sbuf(asc, "srstd0", [P, 1], F32), sbuf(asc, "srstd1", [P, 1], F32)]
                snm = [sbuf(asc, "snm0", [P, 1], F32), sbuf(asc, "snm1", [P, 1], F32)]
                sm2 = sbuf(asc, "sm2", [P, 1], F32)
                smm = sbuf(asc, "smm", [P, 1], F32)
                rm1 = sbuf(asc, "rm1", [P, 1], F32)
                rm2 = sbuf(asc, "rm2", [P, 1], F32)
                teq = sbuf(asc, "teq", [P, 8], F32)
                pfa = sbuf(asc, "pfa", [P, 8], F32)
                pfb = sbuf(asc, "pfb", [P, 8], F32)
                le8 = sbuf(asc, "le8", [P, 8], F32)
                msk2 = sbuf(asc, "msk2", [P, 8], F32)
                d21 = sbuf(asc, "d21", [P, 1], F32)
                e2b = sbuf(asc, "e2b", [P, 1], F32)
                l1b = sbuf(asc, "l1b", [P, 1], F32)

                tA_ev = {}
                tA_rt = {}
                tA_cmp = {}
                tA_nrm = {}

                def fV(emit):
                    sVa.wait(v)
                    return sVa.inc(emit())

                def fS(emit):
                    sSa.wait(sc)
                    return sSa.inc(emit())

                for i in range(NTILE):
                    b, t0 = i // (T // P), (i % (T // P)) * P
                    q = i % 2
                    if i >= 2:
                        sVa.wait(sy, tA_cmp[i - 2])
                    sy.dma_start(out=xt[q][:, :], in_=x_in.ap()[b, t0:t0 + P, :]).then_inc(sXL[q].h, 16)
                    sXL[q].n += 16
                    tok_xl = sXL[q].n
                    sXL[q].wait(v, tok_xl)
                    tok_sum = sVa.inc(v.tensor_reduce(ssum[q][:, :], xt[q][:, :], AX.X, ALU.add))
                    sXL[q].wait(sc, tok_xl)
                    tok_sq = sSa.inc(sc.activation(sqd[:, :], xt[q][:, :], AF.Square, accum_out=ssq[q][:, :]))
                    sVa.wait(v, tok_sum)
                    sSa.wait(v, tok_sq)
                    fV(lambda: v.tensor_scalar(smean[q][:, :], ssum[q][:, :], 1.0 / D, None, ALU.mult, ALU.bypass))
                    fV(lambda: v.tensor_scalar(sm2[:, :], ssq[q][:, :], 1.0 / D, None, ALU.mult, ALU.bypass))
                    fV(lambda: v.tensor_tensor(smm[:, :], smean[q][:, :], smean[q][:, :], ALU.mult))
                    fV(lambda: v.tensor_tensor(svar[q][:, :], sm2[:, :], smm[:, :], ALU.subtract))
                    tok_var = fV(lambda: v.tensor_scalar(svar[q][:, :], svar[q][:, :], EPS, None,
                                                         ALU.add, ALU.bypass))
                    sVa.wait(sc, tok_var)
                    fS(lambda: sc.activation(l1b[:, :], svar[q][:, :], AF.Ln))
                    tok_rstd = fS(lambda: sc.activation(srstd[q][:, :], l1b[:, :], AF.Exp, scale=-0.5))
                    sSa.wait(v, tok_rstd)
                    tok_nm = fV(lambda: v.scalar_tensor_tensor(snm[q][:, :], smean[q][:, :], -1.0,
                                                               srstd[q][:, :], ALU.mult, ALU.mult))
                    sVa.wait(v, tok_nm)
                    tok_xn = sVa.inc(v.tensor_scalar(xn_f[q][:, :], xt[q][:, :],
                                                     srstd[q][:, :], snm[q][:, :], ALU.mult, ALU.add))
                    tA_cmp[i] = tok_xn
                    tA_nrm[i] = tok_xn
                    sVa.wait(t, tok_xn)
                    if i >= 1:
                        sVa.wait(t, tA_ev[i - 1])
                    for c in range(NCH):
                        ins = t.transpose(pb[c // 4][:, (c % 4) * P:(c % 4 + 1) * P],
                                          xn_f[q][:, c * P:(c + 1) * P], id_s[:, :])
                    tok_tr = sTa.inc(ins)
                    sTa.wait(v, tok_tr)
                    v.tensor_copy(xnT_f[q][:, 0:4, :], pb[0][:, :])
                    tok_ev = sVa.inc(v.tensor_copy(xnT_f[q][:, 4:8, :], pb[1][:, :]))
                    tA_ev[i] = tok_ev
                    sVa.wait(sc, tok_ev)
                    sXnW[q].wait(sc)
                    tok_cst = sSa.inc(sc.copy(xnT_b[q][:, :, :], xnT_f[q][:, :, :]))
                    sSa.wait(sy, tok_cst)
                    sy.dma_start(out=xnsc.ap()[b].rearrange("c p t -> p c t")[:, :, t0:t0 + P],
                                 in_=xnT_b[q][:, :, :]).then_inc(sXnW[q].h, 16)
                    sXnW[q].n += 16
                    sVa.wait(t, tok_ev)
                    sConst.wait(t, tok_const)
                    if i >= 1:
                        sVa.wait(t, tA_rt[i - 1])
                    for c in range(NCH):
                        ins = t.matmul(pb[2][:, 0:8], xnT_f[q][:, c, :], wrout_s[:, c, :],
                                       start=(c == 0), stop=(c == NCH - 1))
                    tok_rmm = sTa.inc(ins)
                    sTa.wait(v, tok_rmm)
                    fV(lambda: v.reduce_max(rm1[:, :], pb[2][:, 0:8], AX.X))
                    fV(lambda: v.tensor_scalar(teq[:, :], pb[2][:, 0:8], rm1[:, :], None, ALU.is_ge, ALU.bypass))
                    fV(lambda: v.tensor_copy(pfa[:, 0:1], teq[:, 0:1]))
                    fV(lambda: v.tensor_tensor(pfa[:, 1:8], teq[:, 1:8], teq[:, 0:7], ALU.add))
                    fV(lambda: v.tensor_copy(pfb[:, 0:2], pfa[:, 0:2]))
                    fV(lambda: v.tensor_tensor(pfb[:, 2:8], pfa[:, 2:8], pfa[:, 0:6], ALU.add))
                    fV(lambda: v.tensor_copy(pfa[:, 0:4], pfb[:, 0:4]))
                    fV(lambda: v.tensor_tensor(pfa[:, 4:8], pfb[:, 4:8], pfb[:, 0:4], ALU.add))
                    fV(lambda: v.tensor_scalar(le8[:, :], pfa[:, :], 1.0, None, ALU.is_le, ALU.bypass))
                    fV(lambda: v.tensor_tensor(mask_s[:, i, 0:8], teq[:, :], le8[:, :], ALU.mult))
                    fV(lambda: v.scalar_tensor_tensor(msk2[:, :], mask_s[:, i, 0:8], -1e30, pb[2][:, 0:8],
                                                      ALU.mult, ALU.add))
                    fV(lambda: v.reduce_max(rm2[:, :], msk2[:, :], AX.X))
                    fV(lambda: v.tensor_scalar(teq[:, :], msk2[:, :], rm2[:, :], None, ALU.is_ge, ALU.bypass))
                    fV(lambda: v.tensor_copy(pfa[:, 0:1], teq[:, 0:1]))
                    fV(lambda: v.tensor_tensor(pfa[:, 1:8], teq[:, 1:8], teq[:, 0:7], ALU.add))
                    fV(lambda: v.tensor_copy(pfb[:, 0:2], pfa[:, 0:2]))
                    fV(lambda: v.tensor_tensor(pfb[:, 2:8], pfa[:, 2:8], pfa[:, 0:6], ALU.add))
                    fV(lambda: v.tensor_copy(pfa[:, 0:4], pfb[:, 0:4]))
                    fV(lambda: v.tensor_tensor(pfa[:, 4:8], pfb[:, 4:8], pfb[:, 0:4], ALU.add))
                    fV(lambda: v.tensor_scalar(le8[:, :], pfa[:, :], 1.0, None, ALU.is_le, ALU.bypass))
                    tok_m2 = fV(lambda: v.tensor_tensor(mask_s[:, i, 8:16], teq[:, :], le8[:, :], ALU.mult))
                    tA_rt[i] = tok_m2
                    tok_d21 = fV(lambda: v.tensor_tensor(d21[:, :], rm2[:, :], rm1[:, :], ALU.subtract))
                    sVa.wait(sc, tok_d21)
                    fS(lambda: sc.activation(e2b[:, :], d21[:, :], AF.Exp))
                    fS(lambda: sc.activation(l1b[:, :], e2b[:, :], AF.Ln, bias=1.0))
                    tok_t1 = fS(lambda: sc.activation(topv_s[:, i, 0:1], l1b[:, :], AF.Exp, scale=-1.0))
                    sSa.wait(v, tok_t1)
                    tok_t2 = fV(lambda: v.tensor_tensor(topv_s[:, i, 1:2], e2b[:, :], topv_s[:, i, 0:1], ALU.mult))
                    sVa.wait(g, tok_m2)
                    tok_tot = sGa.inc(g.tensor_reduce(tot_s[:, :], mask_s[:, i, :], AX.C, ALU.add))
                    sGa.wait(v, tok_tot)
                    if i == 0:
                        fV(lambda: v.memset(off_s[:, i, :], 0.0))
                        tok_run = fV(lambda: v.tensor_copy(run_s[:, :], tot_s[:, :]))
                    else:
                        fV(lambda: v.tensor_copy(off_s[:, i, :], run_s[:, :]))
                        tok_run = fV(lambda: v.tensor_tensor(run_s[:, :], run_s[:, :], tot_s[:, :], ALU.add))
                    tok_run_last = tok_run

                sVa.wait(g, tok_run_last)
                g.dma_start(out=ccin.ap(), in_=run_s[:, :]).then_inc(sCC.h, 16)
                sCC.n += 16
                sCC.wait(g)
                sCCC.inc(g.collective_compute("AllGather", ALU.bypass,
                                              replica_groups=[list(range(8))],
                                              ins=[ccin.ap()], outs=[ccout.ap()]), 1)
                sCCC.wait(g)
                g.dma_start(out=agb_s[:, :], in_=ccout.ap()).then_inc(sCC.h, 16)
                sCC.n += 16
                tok_agb = sCC.n
                sCC.wait(t, tok_agb)
                sConst.wait(t, tok_const)
                tok_coffmm = sTa.inc(t.matmul(pb[3][0:1, 0:16], pref_s[:, :], agb_s[:, :],
                                              start=True, stop=True))
                sTa.wait(v, tok_coffmm)
                tok_coff = sVa.inc(v.tensor_copy(coff_s[:, :], pb[3][0:1, 0:16]))

                wt8_s = sbuf(asc, "wt8_s", [NE, NTOK], F32)
                moff = sbuf(asc, "moff", [1, 16], F32)
                keep_s = sbuf(asc, "keep_s", [P, 16], F32)
                wtile = sbuf(asc, "wtile", [P, 8], F32)
                lem = sbuf(asc, "lem", [P, 16], F32)
                tok_rk_prev = None
                tok_wtr_prev = None
                sSa.wait(v)
                for j in range(NTILE):
                    sVa.wait(v, tok_coff)
                    tok_moff = sVa.inc(v.tensor_tensor(moff[:, :], off_s[:, j, :], coff_s[:, :], ALU.add))
                    sVa.wait(t, tok_moff)
                    if tok_rk_prev is not None:
                        sVa.wait(t, tok_rk_prev)
                    t.matmul(pb[3][:, 0:16], triu_s[:, :], mask_s[:, j, :], start=True, stop=False)
                    tok_rk = sTa.inc(t.matmul(pb[3][:, 0:16], ones_s[:, :], moff[:, :],
                                              start=False, stop=True))
                    sTa.wait(v, tok_rk)
                    fV(lambda: v.tensor_scalar(lem[:, :], pb[3][:, 0:16], CAP, None, ALU.is_le, ALU.bypass))
                    tok_keep = fV(lambda: v.tensor_tensor(keep_s[:, :], lem[:, :], mask_s[:, j, :], ALU.mult))
                    tok_rk_prev = tok_keep
                    fV(lambda: v.tensor_scalar(wtile[:, :], keep_s[:, 0:8], topv_s[:, j, 0:1], None, ALU.mult, ALU.bypass))
                    tok_w = fV(lambda: v.scalar_tensor_tensor(wtile[:, :], keep_s[:, 8:16],
                                                              topv_s[:, j, 1:2], wtile[:, :], ALU.mult, ALU.add))
                    sVa.wait(t, tok_w)
                    if tok_wtr_prev is not None:
                        sVa.wait(t, tok_wtr_prev)
                    tok_wtr = sTa.inc(t.transpose(pb[4][0:8, 0:P], wtile[:, :], id_s[:, :]))
                    sTa.wait(v, tok_wtr)
                    tok_wev = sVa.inc(v.tensor_copy(wt8_s[:, j * P:(j + 1) * P], pb[4][0:8, 0:P]))
                    tok_wtr_prev = tok_wev
                sVa.wait(sy, tok_wev)
                sy.dma_start(out=wt8sc.ap(), in_=wt8_s[:, :]).then_inc(sWT8.h, 16)
                sWT8.n += 16
                asc.close()
                sTa.wait(sy)
                sVa.wait(sy)
                sSa.wait(sy)
                sGa.wait(sy)
                sWT8.wait(sy)

                if "B" in PHASES:
                    xw = [sbuf(ab, "xw0", [P, NCH, TCB + 1], BF16),
                          sbuf(ab, "xw1", [P, NCH, TCB + 1], BF16)]
                    xa_t = sbuf(ab, "xa_t", [P, TCB], BF16)
                    mk_b = sbuf(ab, "mk_b", [P, NCH, TCB], BF16)
                    mv_b = sbuf(ab, "mv_b", [P, NCH, TCB], BF16)
                    mr_b = sbuf(ab, "mr_b", [P, NCH, TCB], BF16)
                    ek_w = sbuf(ab, "ek_w", [P, NCH, TCB + 1], F32)
                    ekv_w = sbuf(ab, "ekv_w", [P, NCH, TCB + 1], F32)
                    aex_b = sbuf(ab, "aex_b", [P, NCH, TCB], F32)
                    bex_b = sbuf(ab, "bex_b", [P, NCH, TCB], F32)
                    sra_b = sbuf(ab, "sra_b", [P, NCH, TCB], F32)
                    rw_b = sbuf(ab, "rw_b", [P, NCH, TCB], BF16)
                    sev_s = sbuf(ab, "sev_s", [P, NCH, TCB], F32)
                    carA = sbuf(ab, "carA", [P, NCH], F32)
                    carB = sbuf(ab, "carB", [P, NCH], F32)
                    ones_tc = sbuf(ab, "ones_tc", [P, TCB], F32)
                    jnk_s = sbuf(ab, "jnk_s", [P, 1], F32)
                    numb_b = sbuf(ab, "numb_b", [P, NCH, TCB], F32)
                    denb_b = sbuf(ab, "denb_b", [P, NCH, TCB], F32)
                    swa = sbuf(ab, "swa", [P, NCH, 2 * TCB], F32)
                    swb = sbuf(ab, "swb", [P, NCH, 2 * TCB], F32)
                    ewpow = sbuf(ab, "ewpow", [P, NCH, TCB], F32)
                    ew2k = sbuf(ab, "ew2k", [P, NCH, 8], F32)

                    sWT8.wait(v)
                    sXnW[0].wait(v)
                    sXnW[1].wait(v)
                    sSa.wait(v, tok_ewu)
                    v.memset(ones_tc[:, :], 1.0)
                    fB = lambda emit: (sVb.wait(v), sVb.inc(emit()))[1]
                    fB(lambda: v.tensor_copy(ew2k[:, :, 0], ew_s[:, :]))
                    for k in range(1, 8):
                        fB(lambda k=k: v.tensor_tensor(ew2k[:, :, k:k + 1], ew2k[:, :, k - 1:k],
                                                       ew2k[:, :, k - 1:k], ALU.mult))
                    fB(lambda: v.memset(swa[:, :, :], 1.0))
                    fB(lambda: v.memset(swb[:, :, :], 1.0))
                    for c in range(NCH):
                        fB(lambda c=c: v.tensor_scalar(swa[:, c, TCB:2 * TCB], ones_tc[:, :],
                                                       ew_s[:, c:c + 1], None, ALU.mult, ALU.bypass))
                    cur, nxt = swa, swb
                    for k in range(8):
                        sh = 1 << k
                        fB(lambda cur=cur, nxt=nxt, sh=sh: v.tensor_tensor(
                            nxt[:, :, TCB:2 * TCB], cur[:, :, TCB - sh:2 * TCB - sh],
                            cur[:, :, TCB:2 * TCB], ALU.mult))
                        cur, nxt = nxt, cur
                    tok_ewb = fB(lambda cur=cur: v.tensor_copy(ewpow[:, :, :], cur[:, :, TCB:2 * TCB]))
                    fB(lambda: v.memset(swa[:, :, 0:TCB], 0.0))
                    tok_pad = fB(lambda: v.memset(swb[:, :, 0:TCB], 0.0))

                    tB_mix_done = {}
                    tB_ek_last = 0
                    tB_ekv_last = 0
                    tB_sr_last = 0
                    for it in range(BC * NB_B):
                        b, j = it // NB_B, it % NB_B
                        t0 = j * TCB
                        q = it % 2
                        if B_SERIAL:
                            for eng in (sy, t, v, sc):
                                sVb.wait(eng)
                                sSb.wait(eng)
                                sTb.wait(eng)
                        if it >= 2:
                            sVb.wait(sy, tB_mix_done[q])
                        sXnW[0].wait(sy)
                        sXnW[1].wait(sy)
                        if j == 0:
                            sy.dma_start(out=xw[q][:, :, 1:TCB + 1],
                                         in_=xnsc.ap()[b].rearrange("c p t -> p c t")[:, :, 0:TCB]
                                         ).then_inc(sXW[q].h, 16)
                        else:
                            sy.dma_start(out=xw[q][:, :, 0:TCB + 1],
                                         in_=xnsc.ap()[b].rearrange("c p t -> p c t")[:, :, t0 - 1:t0 + TCB]
                                         ).then_inc(sXW[q].h, 16)
                        sXW[q].n += 16
                        tok_xw = sXW[q].n
                        sXW[q].wait(v, tok_xw)
                        if j == 0:
                            v.memset(xw[q][:, :, 0:1], 0.0)
                        for (mm_b, tm, tm1) in ((mk_b, tmk_s, tmk1_s), (mv_b, tmv_s, tmv1_s),
                                                (mr_b, tmr_s, tmr1_s)):
                            for c in range(NCH):
                                v.tensor_scalar(xa_t[:, :], xw[q][:, c, 1:TCB + 1], tm[:, c:c + 1],
                                                None, ALU.mult, ALU.bypass)
                                ins = v.scalar_tensor_tensor(mm_b[:, c, :], xw[q][:, c, 0:TCB],
                                                             tm1[:, c:c + 1], xa_t[:, :], ALU.mult, ALU.add)
                        tok_mix = sVb.inc(ins)
                        tB_mix_done[q] = tok_mix
                        sVb.wait(v, tok_mix)
                        tok_mixf = sVb.inc(v.tensor_copy(jnk_s[:, :], mr_b[:, NCH - 1, TCB - 1:TCB]))
                        sVb.wait(t, tok_mixf)
                        sPrepR.wait(t, tok_rwkv_w)
                        tok_kmm = [None] * NCH
                        tok_vmm = [None] * NCH
                        tok_rmm_ = [None] * (NCH + 1)
                        tok_ek = [None] * NCH
                        tok_ekv = [None] * NCH
                        tok_sr1 = [None] * NCH
                        for mc in range(NCH):
                            bk = pb[mc % 2]
                            if it > 0 or mc >= 2:
                                sSb.wait(t, tok_ek[mc - 2] if mc >= 2 else tB_ek_last)
                            for kc in range(NCH):
                                ins = t.matmul(bk[:, 0:TCB], wk_b[:, kc, mc * P:(mc + 1) * P],
                                               mk_b[:, kc, :], start=(kc == 0), stop=(kc == NCH - 1))
                            tok_kmm[mc] = sTb.dinc(t)
                            if mc >= 1:
                                sTb.wait(sc, tok_kmm[mc])
                                tok_ek[mc - 1] = sSb.inc(sc.activation(ek_w[:, mc - 1, 1:TCB + 1],
                                                                       pb[(mc - 1) % 2][:, 0:TCB], AF.Exp))
                        for mc in range(NCH):
                            bk = pb[2 + mc % 2]
                            if it > 0 or mc >= 2:
                                sVb.wait(t, tok_ekv[mc - 2] if mc >= 2 else tB_ekv_last)
                            for kc in range(NCH):
                                ins = t.matmul(bk[:, 0:TCB], wv_b[:, kc, mc * P:(mc + 1) * P],
                                               mv_b[:, kc, :], start=(kc == 0), stop=(kc == NCH - 1))
                            tok_vmm[mc] = sTb.dinc(t)
                            if mc == 0:
                                sTb.wait(sc, tok_vmm[0])
                                tok_ek[NCH - 1] = sSb.inc(sc.activation(ek_w[:, NCH - 1, 1:TCB + 1],
                                                                        pb[(NCH - 1) % 2][:, 0:TCB], AF.Exp))
                            else:
                                sTb.wait(v, tok_vmm[mc])
                                sSb.wait(v, tok_ek[mc - 1])
                                tok_ekv[mc - 1] = sVb.inc(v.tensor_tensor(ekv_w[:, mc - 1, 1:TCB + 1],
                                                                          ek_w[:, mc - 1, 1:TCB + 1],
                                                                          pb[2 + (mc - 1) % 2][:, 0:TCB],
                                                                          ALU.mult))
                        tB_ek_last = tok_ek[NCH - 1]
                        for mc in range(NCH + 1):
                            if mc < NCH:
                                bk = pb[4 + mc % 2]
                                if it > 0 or mc >= 2:
                                    sSb.wait(t, tok_sr1[mc - 2] if mc >= 2 else tB_sr_last)
                                for kc in range(NCH):
                                    ins = t.matmul(bk[:, 0:TCB], wr_b[:, kc, mc * P:(mc + 1) * P],
                                                   mr_b[:, kc, :], start=(kc == 0), stop=(kc == NCH - 1))
                                tok_rmm_[mc] = sTb.dinc(t)
                            else:
                                sSb.wait(t, tok_ek[NCH - 2])
                                ins = t.matmul(pb[0][:, 0:TCB], wr_b[:, 0, 0:P], mr_b[:, 0, :],
                                               start=True, stop=True)
                                tok_rmm_[mc] = sTb.dinc(t)
                            if mc == 0:
                                sTb.wait(v, tok_rmm_[0])
                                sSb.wait(v, tok_ek[NCH - 1])
                                tok_ekv[NCH - 1] = sVb.inc(v.tensor_tensor(ekv_w[:, NCH - 1, 1:TCB + 1],
                                                                           ek_w[:, NCH - 1, 1:TCB + 1],
                                                                           pb[2 + (NCH - 1) % 2][:, 0:TCB],
                                                                           ALU.mult))
                            else:
                                sTb.wait(sc, tok_rmm_[mc])
                                tok_sr1[mc - 1] = sSb.inc(sc.activation(sra_b[:, mc - 1, :],
                                                                        pb[4 + (mc - 1) % 2][:, 0:TCB],
                                                                        AF.Exp, scale=-1.0))
                        tB_ekv_last = tok_ekv[NCH - 1]
                        tB_sr_last = tok_sr1[NCH - 1]
                        sSb.wait(sc)
                        sSb.inc(sc.activation(denb_b[:, :, :], sra_b[:, :, :], AF.Ln, bias=1.0))
                        sSb.wait(sc)
                        tok_sr = sSb.inc(sc.activation(sra_b[:, :, :], denb_b[:, :, :], AF.Exp, scale=-1.0))
                        scan_wait = tok_ekv[NCH - 1]
                        if j == 0:
                            v.memset(ek_w[:, :, 0:1], 0.0)
                            v.memset(ekv_w[:, :, 0:1], 0.0)
                            v.memset(carA[:, :], 0.0)
                            tok_bnd = sVb.inc(v.memset(carB[:, :], 0.0))
                            scan_wait = tok_bnd
                        sVb.wait(v, scan_wait)
                        for (srcw, dst, car) in ((ekv_w, aex_b, carA), (ek_w, bex_b, carB)):
                            sVb.wait(v)
                            sVb.inc(v.tensor_copy(swa[:, :, TCB:2 * TCB], srcw[:, :, 0:TCB]))
                            cur, nxt = swa, swb
                            for k in range(8):
                                sh = 1 << k
                                sVb.wait(v)
                                for c in range(NCH):
                                    ins = v.scalar_tensor_tensor(
                                        nxt[:, c, TCB:2 * TCB], cur[:, c, TCB - sh:2 * TCB - sh],
                                        ew2k[:, c, k:k + 1], cur[:, c, TCB:2 * TCB],
                                        ALU.mult, ALU.add)
                                sVb.inc(ins)
                                cur, nxt = nxt, cur
                            sVb.wait(v)
                            for c in range(NCH):
                                ins = v.scalar_tensor_tensor(
                                    dst[:, c, :], ewpow[:, c, :], car[:, c:c + 1],
                                    cur[:, c, TCB:2 * TCB], ALU.mult, ALU.add)
                            sVb.inc(ins)
                        tok_scan = sVb.inc(v.tensor_copy(jnk_s[:, :], bex_b[:, NCH - 1, TCB - 1:TCB]))
                        if DEBUG_B3:
                            sVb.wait(sy, tok_scan)
                            sy.dma_start(out=dbg["ekvall"].ap()[it], in_=ekv_w[:, :, :]).then_inc(sDbgB.h, 16)
                            sDbgB.n += 16
                            sy.dma_start(out=dbg["ekall"].ap()[it], in_=ek_w[:, :, :]).then_inc(sDbgB.h, 16)
                            sDbgB.n += 16
                            sy.dma_start(out=dbg["aexall"].ap()[it], in_=aex_b[:, :, :]).then_inc(sDbgB.h, 16)
                            sDbgB.n += 16
                            sy.dma_start(out=dbg["bexall"].ap()[it], in_=bex_b[:, :, :]).then_inc(sDbgB.h, 16)
                            sDbgB.n += 16
                        sVb.wait(v, tok_scan)
                        v.tensor_copy(carA[:, :], aex_b[:, :, TCB - 1])
                        v.tensor_copy(carB[:, :], bex_b[:, :, TCB - 1])
                        v.tensor_copy(ek_w[:, :, 0:1], ek_w[:, :, TCB:TCB + 1])
                        tok_car = sVb.inc(v.tensor_copy(ekv_w[:, :, 0:1], ekv_w[:, :, TCB:TCB + 1]))
                        if DEBUG_B and it == IT_DUMP:
                            sVb.wait(g, tok_car)
                            g.dma_start(out=dbg["aex"].ap(), in_=aex_b[:, :, :]).then_inc(sDbgB.h, 16)
                            sDbgB.n += 16
                            g.dma_start(out=dbg["bex"].ap(), in_=bex_b[:, :, :]).then_inc(sDbgB.h, 16)
                            sDbgB.n += 16
                            sDbgB.wait(v)
                        for c in range(NCH):
                            v.scalar_tensor_tensor(numb_b[:, c, :], ekv_w[:, c, 1:TCB + 1],
                                                   eu_s[:, c:c + 1], aex_b[:, c, :], ALU.mult, ALU.add)
                            ins = v.scalar_tensor_tensor(denb_b[:, c, :], ek_w[:, c, 1:TCB + 1],
                                                         eu_s[:, c:c + 1], bex_b[:, c, :], ALU.mult, ALU.add)
                        tok_numden = sVb.inc(ins)
                        sVb.wait(v, tok_numden)
                        tok_rden = sVb.inc(v.reciprocal(swb[:, :, TCB:2 * TCB], denb_b[:, :, :]))
                        sVb.wait(v, tok_rden)
                        v.tensor_tensor(aex_b[:, :, :], numb_b[:, :, :], swb[:, :, TCB:2 * TCB], ALU.mult)
                        sSb.wait(v, tok_sr)
                        sVb.inc(v.tensor_tensor(rw_b[:, :, :], aex_b[:, :, :], sra_b[:, :, :],
                                                ALU.mult))
                        sVb.wait(v)
                        tok_rw = sVb.inc(v.tensor_copy(jnk_s[:, :], rw_b[:, NCH - 1, TCB - 1:TCB]))
                        if DEBUG_B3:
                            sVb.wait(sy, tok_rw)
                            sy.dma_start(out=dbg["rwall"].ap()[it], in_=rw_b[:, :, :]).then_inc(sDbgB.h, 16)
                            sDbgB.n += 16
                            sy.dma_start(out=dbg["wkvall"].ap()[it], in_=aex_b[:, :, :]).then_inc(sDbgB.h, 16)
                            sDbgB.n += 16
                            sy.dma_start(out=dbg["sraall"].ap()[it], in_=sra_b[:, :, :]).then_inc(sDbgB.h, 16)
                            sDbgB.n += 16
                        sVb.wait(t, tok_rw)
                        if it >= 1:
                            sSw[0].wait(t)
                        tok_omm = [None] * (NCH + 1)
                        tok_oev = [None] * NCH
                        for mc in range(NCH + 1):
                            if mc < NCH:
                                bk = pb[6 + mc % 2]
                                if mc >= 2:
                                    sVb.wait(t, tok_oev[mc - 2])
                                for kc in range(NCH):
                                    ins = t.matmul(bk[:, 0:TCB], wo_b[:, kc, mc * P:(mc + 1) * P],
                                                   rw_b[:, kc, :], start=(kc == 0), stop=(kc == NCH - 1))
                                tok_omm[mc] = sTb.dinc(t)
                            else:
                                ins = t.matmul(pb[5][:, 0:TCB], wo_b[:, 0, 0:P], rw_b[:, 0, :],
                                               start=True, stop=True)
                                tok_omm[mc] = sTb.dinc(t)
                            if mc >= 1:
                                sTb.wait(v, tok_omm[mc])
                                tok_oev[mc - 1] = sVb.inc(v.tensor_copy(sev_s[:, mc - 1, :],
                                                                        pb[6 + (mc - 1) % 2][:, 0:TCB]))
                        sVb.wait(sy, tok_oev[NCH - 1])
                        sy.dma_start(out=ssc.ap()[b].rearrange("c p t -> p c t")[:, :, t0:t0 + TCB],
                                     in_=sev_s[:, :, :]).then_inc(sSw[0].h, 16)
                        sSw[0].n += 16
                        if DEBUG_B and it == IT_DUMP:
                            sVb.wait(g)
                            sSb.wait(g)
                            sTb.wait(g)
                            for nm, buf in [("mk", mk_b), ("ek", ek_w), ("ekv", ekv_w),
                                            ("den", den_b), ("sra", sra_b), ("rw", rw_b),
                                            ("sev", sev_s), ("carA", carA), ("carB", carB)]:
                                ap_src = buf[:, :] if nm in ("carA", "carB") else buf[:, :, :]
                                g.dma_start(out=dbg[nm].ap(), in_=ap_src).then_inc(sDbgB.h, 16)
                                sDbgB.n += 16
                            sDbgB.wait(v)
                            sDbgB.wait(sc)
                            sDbgB.wait(t)

            if "C" in PHASES:
                with ExitStack() as cs:
                    xnblk = sbuf(cs, "xnblk", [P, NCH, BLK], BF16)
                    h_sb = sbuf(cs, "h_sb", [P, FF // P, BLK], BF16)
                    acc = sbuf(cs, "acc", [P, NCH, BLK], F32)
                    w1ch = [sbuf(cs, "w1ch0", [P, NCH, 512], BF16),
                            sbuf(cs, "w1ch1", [P, NCH, 512], BF16)]
                    w2ch = [sbuf(cs, "w2ch0", [P, 4, D], BF16),
                            sbuf(cs, "w2ch1", [P, 4, D], BF16)]
                    wb_sb = sbuf(cs, "wb_sb", [P, 2, 512], F32)
                    wrow = sbuf(cs, "wrow", [1, 2, BLK], F32)
                    wt8b = sbuf(cs, "wt8b", [NE, 2, 512], F32)
                    ytmp = sbuf(cs, "ytmp", [P, 512], F32)

                    for eng in (sy, t, v, sc):
                        sVb.wait(eng)
                        sSb.wait(eng)
                        sTb.wait(eng)
                        sSw[0].wait(eng)
                        sSw[1].wait(eng)
                        sVa.wait(eng)
                        sWT8.wait(eng)
                    sPrep.wait(sy)
                    if DEBUG and "B" in PHASES:
                        sSw[0].wait(g)
                        sSw[1].wait(g)
                        g.dma_start(out=dbg["s2"].ap(), in_=ssc.ap()).then_inc(sDbgB.h, 16)
                        sDbgB.n += 16

                    tok_yev = {}
                    tok_yev01 = {}
                    tok_gelu = {}
                    tok_b2cp = None
                    tok_w1 = {}
                    tok_w2 = {}
                    tok_mm1g = {}
                    tok_w2mm = {}
                    tok_wbmm_by_e = {}

                    for blk in range(NBLK):
                        b, t0 = blk // (NBLK // BC), (blk % (NBLK // BC)) * BLK
                        gtok0 = blk * BLK
                        if blk > 0:
                            sTc.wait(sy, tok_mm1_last)
                            sMw.wait(sy, 16 * blk)
                        sy.dma_start(out=xnblk[:, :, :],
                                     in_=xnsc.ap()[b].rearrange("c p t -> p c t")[:, :, t0:t0 + BLK]
                                     ).then_inc(sXnB.h, 16)
                        sXnB.n += 16
                        tok_xnb = sXnB.n
                        sy.dma_start(out=wt8b[:, :, :],
                                     in_=wt8sc.ap()[:, gtok0:gtok0 + BLK].rearrange("e (n f) -> e n f", n=2)
                                     ).then_inc(sWt8b.h, 16)
                        sWt8b.n += 16
                        tok_wt8b = sWt8b.n
                        sWt8b.wait(t, tok_wt8b)
                        sConst.wait(t, tok_const)
                        if blk > 0:
                            sVc.wait(t, tok_b2_free)
                        tok_b2mm = {}
                        for nh in range(2):
                            for mcd in range(NCH):
                                ins = t.matmul(pb[7][:, 0:512], b2_s[:, mcd * P:(mcd + 1) * P],
                                               wt8b[:, nh, :], start=True, stop=True)
                                tok_b2mm[(nh, mcd)] = sTc.inc(ins)
                                sTc.wait(v, tok_b2mm[(nh, mcd)])
                                if tok_b2cp is not None:
                                    sMw.wait(v, 16 * blk)
                                ins2 = v.tensor_copy(acc[:, mcd, nh * 512:(nh + 1) * 512], pb[7][:, 0:512])
                                tok_b2cp = sVc.inc(ins2)
                                sVc.wait(t, tok_b2cp)
                        tok_b2_free = tok_b2cp

                        for e in range(NE):
                            qe = e % 2
                            if e >= 2 or blk > 0:
                                sTc.wait(sy, tok_wbmm_by_e[(e - 2) % NE])
                            sy.dma_start(out=wrow[0:1, qe, :],
                                         in_=wt8sc.ap()[e:e + 1, gtok0:gtok0 + BLK]).then_inc(sWrow.h, 16)
                            sWrow.n += 16
                            tok_wrow = sWrow.n
                            gidx = 0
                            for fc in range(NCH):
                                qw = fc % 2
                                key = (blk, e, fc)
                                prev = (blk, e, fc - 2) if fc >= 2 else ((blk, e - 1, NCH - 2 + fc) if e > 0 else ((blk - 1, NE - 1, NCH - 2 + fc) if blk > 0 else None))
                                if prev in tok_mm1g:
                                    sTc.wait(sy, tok_mm1g[prev])
                                sy.dma_start(out=w1ch[qw][:, :, :],
                                             in_=w1sc.ap()[e].rearrange("(kc p) f -> p kc f", p=P)
                                             [:, :, fc * 512:(fc + 1) * 512]).then_inc(sW1c[qw].h, 16)
                                sW1c[qw].n += 16
                                tok_w1[key] = sW1c[qw].n
                                sW1c[qw].wait(t, tok_w1[key])
                                sXnB.wait(t, tok_xnb)
                                for mc in range(4):
                                    for nh in range(2):
                                        hb = pb[gidx % 2]
                                        gk = (blk, e, gidx)
                                        gprev = (blk, e, gidx - 2)
                                        if gidx >= 2 and gprev in tok_gelu:
                                            sGc.wait(t, tok_gelu[gprev])
                                        elif gidx < 2 and (blk, e) != (0, 0):
                                            sVc.wait(t, tok_yev01.get((blk, e - 1, 1) if e > 0 else (blk - 1, NE - 1, 1), 0))
                                            sGc.wait(t)
                                        for kc in range(NCH):
                                            ins = t.matmul(hb[:, 0:512],
                                                           w1ch[qw][:, kc, mc * P:(mc + 1) * P],
                                                           xnblk[:, kc, nh * 512:(nh + 1) * 512],
                                                           start=(kc == 0), stop=(kc == NCH - 1))
                                        tok_mm1g[key] = sTc.inc(ins)
                                        sTc.wait(sc, tok_mm1g[key])
                                        gins = sc.activation(h_sb[:, fc * 4 + mc, nh * 512:(nh + 1) * 512],
                                                             hb[:, 0:512], AF.Gelu,
                                                             bias=b1_s[:, e, fc * 4 + mc:fc * 4 + mc + 1])
                                        tok_gelu[gk] = sGc.inc(gins)
                                        gidx += 1
                            tok_mm1_last = tok_mm1g[(blk, e, NCH - 1)]
                            tok_gelu[(blk, e)] = tok_gelu[(blk, e, 63)]
                            sWrow.wait(t, tok_wrow)
                            if (blk, e) != (0, 0):
                                pk = (blk, e - 1, 1) if e > 0 else (blk - 1, NE - 1, 1)
                                sVc.wait(t, tok_yev.get(pk, 0))
                            for nh in range(2):
                                ins = t.matmul(pb[6][:, 0:512] if nh == 0 else pb[5][:, 0:512],
                                               ones_s[:, :], wrow[0:1, qe, nh * 512:(nh + 1) * 512],
                                               start=True, stop=True)
                            tok_wbmm = sTc.inc(ins)
                            tok_wbmm_by_e[e] = tok_wbmm
                            sTc.wait(v, tok_wbmm)
                            v.tensor_copy(wb_sb[:, 0, :], pb[6][:, 0:512])
                            tok_wb = sVc.inc(v.tensor_copy(wb_sb[:, 1, :], pb[5][:, 0:512]))
                            for nh in range(2):
                                sGc.wait(t, tok_gelu[(blk, e)])
                                sVc.wait(t, tok_wb)
                                pk = (blk, e, nh - 1) if nh > 0 else ((blk, e - 1, 1) if e > 0 else ((blk - 1, NE - 1, 1) if blk > 0 else None))
                                if pk in tok_yev:
                                    sVc.wait(t, tok_yev[pk])
                                for fh in range(NCH):
                                    qw = fh % 2
                                    keyw = (blk, e, nh, fh)
                                    prevw = (blk, e, nh, fh - 2) if fh >= 2 else ((blk, e, nh - 1, NCH - 2 + fh) if nh > 0 else ((blk, e - 1, 1, NCH - 2 + fh) if e > 0 else ((blk - 1, NE - 1, 1, NCH - 2 + fh) if blk > 0 else None)))
                                    if prevw in tok_w2mm:
                                        sTc.wait(sy, tok_w2mm[prevw])
                                    sy.dma_start(out=w2ch[qw][:, :, :],
                                                 in_=w2sc.ap()[e][fh * 512:(fh + 1) * 512, :]
                                                 .rearrange("(a p) d -> p a d", p=P)).then_inc(sW2c[qw].h, 16)
                                    sW2c[qw].n += 16
                                    tok_w2[keyw] = sW2c[qw].n
                                    sW2c[qw].wait(t, tok_w2[keyw])
                                    for ka in range(4):
                                        kf = fh * 4 + ka
                                        for dh in range(2):
                                            for mc in range(4):
                                                ins = t.matmul(pb[dh * 4 + mc][:, 0:512],
                                                               w2ch[qw][:, ka, dh * 512 + mc * P:dh * 512 + (mc + 1) * P],
                                                               h_sb[:, kf, nh * 512:(nh + 1) * 512],
                                                               start=(kf == 0), stop=(kf == FF // P - 1))
                                    tok_w2mm[keyw] = sTc.inc(ins)
                                tok_mm2 = tok_w2mm[(blk, e, nh, NCH - 1)]
                                sTc.wait(v, tok_mm2)
                                sVc.wait(v, tok_wb)
                                first_tok = None
                                for dh in range(2):
                                    for mc in range(4):
                                        v.tensor_tensor(ytmp[:, :], pb[dh * 4 + mc][:, 0:512],
                                                        wb_sb[:, nh, :], ALU.mult)
                                        ins = v.tensor_tensor(acc[:, dh * 4 + mc, nh * 512:(nh + 1) * 512],
                                                              acc[:, dh * 4 + mc, nh * 512:(nh + 1) * 512],
                                                              ytmp[:, :], ALU.add)
                                        if dh == 0 and mc == 1:
                                            first_tok = sVc.inc(ins)
                                tok_ye = sVc.inc(ins)
                                tok_yev[(blk, e, nh)] = tok_ye
                                tok_yev01[(blk, e, nh)] = first_tok
                        sVc.wait(sy, tok_yev[(blk, NE - 1, 1)])
                        sy.dma_start(out=moesc.ap()[blk].rearrange("c p t -> p c t"),
                                     in_=acc[:, :, :]).then_inc(sMw.h, 16)
                        sMw.n += 16

            if "D" in PHASES:
                with ExitStack() as ds:
                    s_blk = sbuf(ds, "s_blk", [P, NCH, BLK], F32)
                    m_blk = sbuf(ds, "m_blk", [P, NCH, BLK], F32)
                    g_blk = sbuf(ds, "g_blk", [P, NCH, BLK], F32)
                    x_blk = sbuf(ds, "x_blk", [P, NCH, D], F32)
                    o_sb = [sbuf(ds, "o_sb0", [P, D], F32), sbuf(ds, "o_sb1", [P, D], F32)]

                    for eng in (sy, t, v, sc):
                        sVc.wait(eng)
                        sGc.wait(eng)
                        sTc.wait(eng)
                        sMw.wait(eng)

                    tok_osb = [0, 0]
                    tok_ost = [0, 0]
                    for blk in range(NBLK):
                        b, t0 = blk // (NBLK // BC), (blk % (NBLK // BC)) * BLK
                        if blk > 0:
                            sTd.wait(sy, tok_tr_all)
                            sSd.wait(sy, tok_g)
                            sVd.wait(sy)
                        sSw[0].wait(sy)
                        sSw[1].wait(sy)
                        sMw.wait(sy)
                        sy.dma_start(out=s_blk[:, :, :],
                                     in_=ssc.ap()[b].rearrange("c p t -> p c t")[:, :, t0:t0 + BLK]
                                     ).then_inc(sLd.h, 16)
                        sLd.n += 16
                        sy.dma_start(out=m_blk[:, :, :],
                                     in_=moesc.ap()[blk].rearrange("c p t -> p c t")).then_inc(sLd.h, 16)
                        sLd.n += 16
                        sy.dma_start(out=x_blk[:, :, :],
                                     in_=x_in.ap()[b, t0:t0 + BLK, :].rearrange("(s p) d -> p s d", p=P)
                                     ).then_inc(sLd.h, 16)
                        sLd.n += 16
                        tok_ld = sLd.n
                        sLd.wait(sc, tok_ld)
                        if blk > 0:
                            sVd.wait(sc, tok_p)
                        tok_g = sSd.inc(sc.activation(g_blk[:, :, :], m_blk[:, :, :], AF.Silu))
                        sSd.wait(v, tok_g)
                        sLd.wait(v, tok_ld)
                        tok_p = sVd.inc(v.tensor_tensor(s_blk[:, :, :], s_blk[:, :, :], g_blk[:, :, :],
                                                        ALU.mult))
                        for sidx in range(NCH):
                            qo = sidx % 2
                            sVd.wait(t, tok_p)
                            if blk > 0 or sidx >= 2:
                                sVd.wait(t, tok_osb[qo])
                            for c in range(NCH):
                                ins = t.transpose(pb[2 * qo + c // 4][:, (c % 4) * P:(c % 4 + 1) * P],
                                                  s_blk[:, c, sidx * P:(sidx + 1) * P], id_s[:, :])
                            tok_tr = sTd.inc(ins)
                            sTd.wait(v, tok_tr)
                            if blk > 0 or sidx >= 2:
                                sOut[qo].wait(v, tok_ost[qo])
                            v.tensor_tensor(o_sb[qo][:, 0:512], pb[2 * qo][:, :],
                                            x_blk[:, sidx, 0:512], ALU.add)
                            ins = v.tensor_tensor(o_sb[qo][:, 512:1024], pb[2 * qo + 1][:, :],
                                                  x_blk[:, sidx, 512:1024], ALU.add)
                            tok_osb[qo] = sVd.inc(ins)
                            sVd.wait(sy, tok_osb[qo])
                            sy.dma_start(out=out.ap()[b, t0 + sidx * P:t0 + (sidx + 1) * P, :],
                                         in_=o_sb[qo][:, :]).then_inc(sOut[qo].h, 16)
                            sOut[qo].n += 16
                            tok_ost[qo] = sOut[qo].n
                        tok_tr_all = tok_tr

            if DEBUG:
                sXnW[0].wait(sy)
                sXnW[1].wait(sy)
                sWT8.wait(sy)
                for eng2 in (t, v, sc):
                    sVa.wait(eng2)
                sy.dma_start(out=dbg["xn"].ap(), in_=xnsc.ap()).then_inc(sOut[0].h, 16)
                sOut[0].n += 16
                if "B" in PHASES:
                    sSw[0].wait(sy)
                    sSw[1].wait(sy)
                    sy.dma_start(out=dbg["s"].ap(), in_=ssc.ap()).then_inc(sOut[0].h, 16)
                    sOut[0].n += 16
                if "C" in PHASES:
                    sMw.wait(sy)
                    sy.dma_start(out=dbg["moe"].ap(), in_=moesc.ap()).then_inc(sOut[0].h, 16)
                    sOut[0].n += 16
                sy.dma_start(out=dbg["wt8"].ap(), in_=wt8sc.ap()).then_inc(sOut[0].h, 16)
                sOut[0].n += 16
            sDbgB.wait(sy)
            sOut[0].wait(sy)
            sOut[1].wait(sy)

    return nc


_NC_CACHE = {}

def _get_nc():
    key = (DEBUG, PHASES)
    if key not in _NC_CACHE:
        _NC_CACHE[key] = build_nc()
    return _NC_CACHE[key]


def kernel(L0_attn_x, L0_attn_time_decay, L0_attn_time_first, L0_attn_time_mix_k,
           L0_attn_time_mix_v, L0_attn_time_mix_r, L0_attn_Wk, L0_attn_Wv,
           L0_attn_Wr, L0_attn_Wo, L0_ffn_Wr, L0_ffn_W1, L0_ffn_b1, L0_ffn_W2,
           L0_ffn_b2, _return_raw=False):
    f32 = np.float32
    x = np.ascontiguousarray(np.asarray(L0_attn_x, dtype=f32))
    wkT = np.ascontiguousarray(np.asarray(L0_attn_Wk, f32).T)
    wvT = np.ascontiguousarray(np.asarray(L0_attn_Wv, f32).T)
    wrT = np.ascontiguousarray(np.asarray(L0_attn_Wr, f32).T)
    woT = np.ascontiguousarray(np.asarray(L0_attn_Wo, f32).T)
    wrout = np.ascontiguousarray(np.asarray(L0_ffn_Wr, f32))
    w1 = np.ascontiguousarray(np.asarray(L0_ffn_W1, f32))
    w2 = np.ascontiguousarray(np.asarray(L0_ffn_W2, f32))
    b1T = np.ascontiguousarray(np.asarray(L0_ffn_b1, f32).reshape(NE, FF // P, P).transpose(0, 2, 1))
    b2 = np.ascontiguousarray(np.asarray(L0_ffn_b2, f32))
    tdT = np.ascontiguousarray(np.asarray(L0_attn_time_decay, f32).reshape(NCH, P).T)
    tfT = np.ascontiguousarray(np.asarray(L0_attn_time_first, f32).reshape(NCH, P).T)
    tmkT = np.ascontiguousarray(np.asarray(L0_attn_time_mix_k, f32).reshape(NCH, P).T)
    tmvT = np.ascontiguousarray(np.asarray(L0_attn_time_mix_v, f32).reshape(NCH, P).T)
    tmrT = np.ascontiguousarray(np.asarray(L0_attn_time_mix_r, f32).reshape(NCH, P).T)
    ident = np.eye(P, dtype=f32)
    triu = np.triu(np.ones((P, P), dtype=f32))
    onesr = np.ones((1, P), dtype=f32)

    nc = _get_nc()
    in_maps = []
    for c in range(8):
        in_maps.append(dict(
            x=np.ascontiguousarray(x[2 * c:2 * c + 2]),
            wkT=wkT, wvT=wvT, wrT=wrT, woT=woT, wrout=wrout,
            w1=w1, w2=w2, b1T=b1T, b2=b2, tdT=tdT, tfT=tfT,
            tmkT=tmkT, tmvT=tmvT, tmrT=tmrT,
            ident=ident, triu=triu, onesr=onesr,
            prefc=np.ascontiguousarray((np.arange(NE) < c).astype(f32).reshape(NE, 1)),
        ))
    res = run_bass_kernel_spmd(nc, in_maps, core_ids=list(range(8)))
    if _return_raw:
        return res
    outp = np.concatenate([np.asarray(res.results[c]["out"]) for c in range(8)], axis=0)
    return outp


# version 05 (final)
# speedup vs baseline: 1.1375x; 1.1375x over previous
import numpy as np
import concourse.bass as bass
import concourse.mybir as mybir
from concourse.bass_utils import run_bass_kernel_spmd
from contextlib import ExitStack

F32 = mybir.dt.float32
BF16 = mybir.dt.bfloat16
AF = mybir.ActivationFunctionType
ALU = mybir.AluOpType
AX = mybir.AxisListType

P = 128
D = 1024
NCH = 8
T = 4096
BC = 2
NTOK = BC * T
NTILE = NTOK // P
TCB = 256
NB_B = T // TCB
BLK = 1024
NBLK = NTOK // BLK
FF = 4096
NE = 8
CAP = 10240.0
EPS = 1e-5

DEBUG = False
DEBUG_B = False
DEBUG_B3 = False
IT_DUMP = 3
B_SERIAL = False
PHASES = "ABCD"


class Sem:

    _DRAIN_ENGINES = ("DVE", "Activation", "Pool")

    def __init__(self, nc, stack, name):
        self.nc = nc
        self.h = stack.enter_context(nc.semaphore(name))
        self.n = 0

    def inc(self, instr, k=1):
        ename = str(getattr(instr.ins, "engine", ""))
        iname = type(instr.ins).__name__
        if any(e in ename for e in self._DRAIN_ENGINES) and "Collective" not in iname:
            eng = {"DVE": self.nc.vector, "Activation": self.nc.scalar,
                   "Pool": self.nc.gpsimd, "PE": self.nc.tensor}[
                       next(e for e in self._DRAIN_ENGINES if e in ename)]
            eng.drain().then_inc(self.h, k)
        else:
            instr.then_inc(self.h, k)
        self.n += k
        return self.n

    def dinc(self, eng, k=1):
        eng.drain().then_inc(self.h, k)
        self.n += k
        return self.n

    def wait(self, eng, tok=None):
        eng.wait_ge(self.h, self.n if tok is None else tok)


def build_nc():
    nc = bass.Bass("TRN2", target_bir_lowering=False)

    x_in = nc.dram_tensor("x", [BC, T, D], F32, kind="ExternalInput")
    wkT = nc.dram_tensor("wkT", [D, D], F32, kind="ExternalInput")
    wvT = nc.dram_tensor("wvT", [D, D], F32, kind="ExternalInput")
    wrT = nc.dram_tensor("wrT", [D, D], F32, kind="ExternalInput")
    woT = nc.dram_tensor("woT", [D, D], F32, kind="ExternalInput")
    wrout = nc.dram_tensor("wrout", [D, NE], F32, kind="ExternalInput")
    w1 = nc.dram_tensor("w1", [NE, D, FF], F32, kind="ExternalInput")
    w2 = nc.dram_tensor("w2", [NE, FF, D], F32, kind="ExternalInput")
    b1T = nc.dram_tensor("b1T", [NE, P, FF // P], F32, kind="ExternalInput")
    b2 = nc.dram_tensor("b2", [NE, D], F32, kind="ExternalInput")
    tdT = nc.dram_tensor("tdT", [P, NCH], F32, kind="ExternalInput")
    tfT = nc.dram_tensor("tfT", [P, NCH], F32, kind="ExternalInput")
    tmkT = nc.dram_tensor("tmkT", [P, NCH], F32, kind="ExternalInput")
    tmvT = nc.dram_tensor("tmvT", [P, NCH], F32, kind="ExternalInput")
    tmrT = nc.dram_tensor("tmrT", [P, NCH], F32, kind="ExternalInput")
    ident = nc.dram_tensor("ident", [P, P], F32, kind="ExternalInput")
    triu = nc.dram_tensor("triu", [P, P], F32, kind="ExternalInput")
    onesr = nc.dram_tensor("onesr", [1, P], F32, kind="ExternalInput")
    prefc = nc.dram_tensor("prefc", [NE, 1], F32, kind="ExternalInput")
    out = nc.dram_tensor("out", [BC, T, D], F32, kind="ExternalOutput")

    xnsc = nc.dram_tensor("xnsc", [BC, NCH, P, T], BF16)
    ssc = nc.dram_tensor("ssc", [BC, NCH, P, T], F32)
    moesc = nc.dram_tensor("moesc", [NBLK, NCH, P, BLK], F32)
    w1sc = nc.dram_tensor("w1sc", [NE, D, FF], BF16)
    w2sc = nc.dram_tensor("w2sc", [NE, FF, D], BF16)
    wt8sc = nc.dram_tensor("wt8sc", [NE, NTOK], F32)
    ccin = nc.dram_tensor("ccin", [1, 16], F32)
    ccout = nc.dram_tensor("ccout", [NE, 16], F32)

    dbg = {}
    if DEBUG_B3:
        dbg["rwall"] = nc.dram_tensor("dbg_rwall", [BC * NB_B, P, NCH, TCB], BF16, kind="ExternalOutput")
        dbg["wkvall"] = nc.dram_tensor("dbg_wkvall", [BC * NB_B, P, NCH, TCB], F32, kind="ExternalOutput")
        dbg["sraall"] = nc.dram_tensor("dbg_sraall", [BC * NB_B, P, NCH, TCB], F32, kind="ExternalOutput")
        dbg["ekvall"] = nc.dram_tensor("dbg_ekvall", [BC * NB_B, P, NCH, TCB + 1], F32, kind="ExternalOutput")
        dbg["ekall"] = nc.dram_tensor("dbg_ekall", [BC * NB_B, P, NCH, TCB + 1], F32, kind="ExternalOutput")
        dbg["aexall"] = nc.dram_tensor("dbg_aexall", [BC * NB_B, P, NCH, TCB], F32, kind="ExternalOutput")
        dbg["bexall"] = nc.dram_tensor("dbg_bexall", [BC * NB_B, P, NCH, TCB], F32, kind="ExternalOutput")
    if DEBUG_B:
        for nm, dt_ in [("mk", BF16), ("ek", F32), ("ekv", F32), ("aex", F32),
                        ("bex", F32), ("den", F32), ("sra", F32), ("rw", BF16),
                        ("sev", F32), ("carA", F32), ("carB", F32)]:
            shape = [P, NCH] if nm in ("carA", "carB") else ([P, NCH, TCB + 1] if nm in ("ek", "ekv") else [P, NCH, TCB])
            dbg[nm] = nc.dram_tensor("dbgB_" + nm, shape, dt_, kind="ExternalOutput")
    if DEBUG:
        dbg["xn"] = nc.dram_tensor("dbg_xn", [BC, NCH, P, T], BF16, kind="ExternalOutput")
        dbg["s"] = nc.dram_tensor("dbg_s", [BC, NCH, P, T], F32, kind="ExternalOutput")
        dbg["s2"] = nc.dram_tensor("dbg_s2", [BC, NCH, P, T], F32, kind="ExternalOutput")
        dbg["moe"] = nc.dram_tensor("dbg_moe", [NBLK, NCH, P, BLK], F32, kind="ExternalOutput")
        dbg["wt8"] = nc.dram_tensor("dbg_wt8", [NE, NTOK], F32, kind="ExternalOutput")

    with ExitStack() as top:
        def sbuf(st, name, shape, dtype):
            return st.enter_context(nc.sbuf_tensor(name, shape, dtype))

        wrout_s = sbuf(top, "wrout_s", [P, NCH, NE], F32)
        b1_s = sbuf(top, "b1_s", [P, NE, FF // P], F32)
        b2_s = sbuf(top, "b2_s", [NE, D], F32)
        td_s = sbuf(top, "td_s", [P, NCH], F32)
        ew_s = sbuf(top, "ew_s", [P, NCH], F32)
        eu_s = sbuf(top, "eu_s", [P, NCH], F32)
        tmk_s = sbuf(top, "tmk_s", [P, NCH], F32)
        tmv_s = sbuf(top, "tmv_s", [P, NCH], F32)
        tmr_s = sbuf(top, "tmr_s", [P, NCH], F32)
        tmk1_s = sbuf(top, "tmk1_s", [P, NCH], F32)
        tmv1_s = sbuf(top, "tmv1_s", [P, NCH], F32)
        tmr1_s = sbuf(top, "tmr1_s", [P, NCH], F32)
        id_s = sbuf(top, "id_s", [P, P], F32)
        triu_s = sbuf(top, "triu_s", [P, P], F32)
        ones_s = sbuf(top, "ones_s", [1, P], F32)
        pref_s = sbuf(top, "pref_s", [NE, 1], F32)
        coff_s = sbuf(top, "coff_s", [1, 16], F32)

        pb = [top.enter_context(nc.psum_tensor(f"pb{i}", [P, 512], F32)) for i in range(8)]

        S = lambda name: Sem(nc, top, name)
        sConst = S("sConst")
        sPrep = S("sPrep")
        sXL = [S("sXL0"), S("sXL1")]
        sVa = S("sVa")
        sSa = S("sSa")
        sTa = S("sTa")
        sGa = S("sGa")
        sXnW = [S("sXnW0"), S("sXnW1")]
        sCC = S("sCC")
        sCCC = S("sCCC")
        sVb = S("sVb")
        sSb = S("sSb")
        sTb = S("sTb")
        sXW = [S("sXW0"), S("sXW1")]
        sSw = [S("sSw0"), S("sSw1")]
        sPrepR = S("sPrepR")
        sWT8 = S("sWT8")
        sXnB = S("sXnB")
        sWt8b = S("sWt8b")
        sWrow = S("sWrow")
        sW1c = [S("sW1c0"), S("sW1c1")]
        sW2c = [S("sW2c0"), S("sW2c1")]
        sTc = S("sTc")
        sGc = S("sGc")
        sVc = S("sVc")
        sMw = S("sMw")
        sDbgB = S("sDbgB")
        sLd = S("sLd")
        sVd = S("sVd")
        sSd = S("sSd")
        sTd = S("sTd")
        sOut = [S("sOut0"), S("sOut1")]

        blk_guard = top.enter_context(nc.Block())

        @blk_guard.gpsimd
        def _(g):
            sy = nc.sync
            v = nc.vector
            sc = nc.scalar
            t = nc.tensor

            for dst, src in [
                (wrout_s[:, :, :], wrout.ap().rearrange("(kc p) e -> p kc e", p=P)),
                (b1_s[:, :, :], b1T.ap().rearrange("e p m -> p e m")),
                (b2_s[:, :], b2.ap()),
                (td_s[:, :], tdT.ap()),
                (eu_s[:, :], tfT.ap()),
                (tmk_s[:, :], tmkT.ap()),
                (tmv_s[:, :], tmvT.ap()),
                (tmr_s[:, :], tmrT.ap()),
                (id_s[:, :], ident.ap()),
                (triu_s[:, :], triu.ap()),
                (ones_s[:, :], onesr.ap()),
                (pref_s[:, :], prefc.ap()),
            ]:
                sy.dma_start(out=dst, in_=src).then_inc(sConst.h, 16)
                sConst.n += 16
            tok_const = sConst.n

            prep_tok_e = []
            with ExitStack() as ab:
                wk_b = sbuf(ab, "wk_b", [P, NCH, D], BF16)
                wv_b = sbuf(ab, "wv_b", [P, NCH, D], BF16)
                wr_b = sbuf(ab, "wr_b", [P, NCH, D], BF16)
                wo_b = sbuf(ab, "wo_b", [P, NCH, D], BF16)
                for dst, src in [(wk_b, wkT), (wv_b, wvT), (wr_b, wrT), (wo_b, woT)]:
                    g.dma_start(out=dst[:, :, :],
                                in_=src.ap().rearrange("(kc p) m -> p kc m", p=P)).then_inc(sPrepR.h, 16)
                    sPrepR.n += 16
                tok_rwkv_w = sPrepR.n
                for e in range(NE):
                    g.dma_start(out=w1sc.ap()[e], in_=w1.ap()[e]).then_inc(sPrep.h, 16)
                    sPrep.n += 16
                    g.dma_start(out=w2sc.ap()[e], in_=w2.ap()[e]).then_inc(sPrep.h, 16)
                    sPrep.n += 16
                    prep_tok_e.append(sPrep.n)

                sConst.wait(v, tok_const)
                v.tensor_scalar(tmk1_s[:, :], tmk_s[:, :], -1.0, 1.0, ALU.mult, ALU.add)
                v.tensor_scalar(tmv1_s[:, :], tmv_s[:, :], -1.0, 1.0, ALU.mult, ALU.add)
                tok_tm1 = sVa.inc(v.tensor_scalar(tmr1_s[:, :], tmr_s[:, :], -1.0, 1.0, ALU.mult, ALU.add))
                sConst.wait(sc, tok_const)
                sc.activation(ew_s[:, :], td_s[:, :], AF.Exp)
                sc.activation(ew_s[:, :], ew_s[:, :], AF.Exp, scale=-1.0)
                tok_ewu = sSa.inc(sc.activation(eu_s[:, :], eu_s[:, :], AF.Exp))

                asc = ExitStack()
                topv_s = sbuf(asc, "topv_s", [P, NTILE, 2], F32)
                mask_s = sbuf(asc, "mask_s", [P, NTILE, 16], F32)
                off_s = sbuf(asc, "off_s", [1, NTILE, 16], F32)
                run_s = sbuf(asc, "run_s", [1, 16], F32)
                tot_s = sbuf(asc, "tot_s", [1, 16], F32)
                agb_s = sbuf(asc, "agb_s", [NE, 16], F32)
                xt = [sbuf(asc, "xt0", [P, D], F32), sbuf(asc, "xt1", [P, D], F32)]
                sqd = sbuf(asc, "sqd", [P, D], F32)
                xn_f = [sbuf(asc, "xnf0", [P, D], F32), sbuf(asc, "xnf1", [P, D], F32)]
                xnT_f = [sbuf(asc, "xnTf0", [P, NCH, P], F32), sbuf(asc, "xnTf1", [P, NCH, P], F32)]
                xnT_b = [sbuf(asc, "xnTb0", [P, NCH, P], BF16), sbuf(asc, "xnTb1", [P, NCH, P], BF16)]
                ssum = [sbuf(asc, "ssum0", [P, 1], F32), sbuf(asc, "ssum1", [P, 1], F32)]
                ssq = [sbuf(asc, "ssq0", [P, 1], F32), sbuf(asc, "ssq1", [P, 1], F32)]
                smean = [sbuf(asc, "smean0", [P, 1], F32), sbuf(asc, "smean1", [P, 1], F32)]
                svar = [sbuf(asc, "svar0", [P, 1], F32), sbuf(asc, "svar1", [P, 1], F32)]
                srstd = [sbuf(asc, "srstd0", [P, 1], F32), sbuf(asc, "srstd1", [P, 1], F32)]
                snm = [sbuf(asc, "snm0", [P, 1], F32), sbuf(asc, "snm1", [P, 1], F32)]
                sm2 = sbuf(asc, "sm2", [P, 1], F32)
                smm = sbuf(asc, "smm", [P, 1], F32)
                rm1 = sbuf(asc, "rm1", [P, 1], F32)
                rm2 = sbuf(asc, "rm2", [P, 1], F32)
                teq = sbuf(asc, "teq", [P, 8], F32)
                pfa = sbuf(asc, "pfa", [P, 8], F32)
                pfb = sbuf(asc, "pfb", [P, 8], F32)
                le8 = sbuf(asc, "le8", [P, 8], F32)
                msk2 = sbuf(asc, "msk2", [P, 8], F32)
                d21 = sbuf(asc, "d21", [P, 1], F32)
                e2b = sbuf(asc, "e2b", [P, 1], F32)
                l1b = sbuf(asc, "l1b", [P, 1], F32)

                tA_ev = {}
                tA_rt = {}
                tA_cmp = {}
                tA_nrm = {}

                def fV(emit):
                    sVa.wait(v)
                    return sVa.inc(emit())

                def fS(emit):
                    sSa.wait(sc)
                    return sSa.inc(emit())

                for i in range(NTILE):
                    b, t0 = i // (T // P), (i % (T // P)) * P
                    q = i % 2
                    if i >= 2:
                        sVa.wait(sy, tA_cmp[i - 2])
                    sy.dma_start(out=xt[q][:, :], in_=x_in.ap()[b, t0:t0 + P, :]).then_inc(sXL[q].h, 16)
                    sXL[q].n += 16
                    tok_xl = sXL[q].n
                    sXL[q].wait(v, tok_xl)
                    tok_sum = sVa.inc(v.tensor_reduce(ssum[q][:, :], xt[q][:, :], AX.X, ALU.add))
                    sXL[q].wait(sc, tok_xl)
                    tok_sq = sSa.inc(sc.activation(sqd[:, :], xt[q][:, :], AF.Square, accum_out=ssq[q][:, :]))
                    sVa.wait(v, tok_sum)
                    sSa.wait(v, tok_sq)
                    fV(lambda: v.tensor_scalar(smean[q][:, :], ssum[q][:, :], 1.0 / D, None, ALU.mult, ALU.bypass))
                    fV(lambda: v.tensor_scalar(sm2[:, :], ssq[q][:, :], 1.0 / D, None, ALU.mult, ALU.bypass))
                    fV(lambda: v.tensor_tensor(smm[:, :], smean[q][:, :], smean[q][:, :], ALU.mult))
                    fV(lambda: v.tensor_tensor(svar[q][:, :], sm2[:, :], smm[:, :], ALU.subtract))
                    tok_var = fV(lambda: v.tensor_scalar(svar[q][:, :], svar[q][:, :], EPS, None,
                                                         ALU.add, ALU.bypass))
                    sVa.wait(sc, tok_var)
                    fS(lambda: sc.activation(l1b[:, :], svar[q][:, :], AF.Ln))
                    tok_rstd = fS(lambda: sc.activation(srstd[q][:, :], l1b[:, :], AF.Exp, scale=-0.5))
                    sSa.wait(v, tok_rstd)
                    tok_nm = fV(lambda: v.scalar_tensor_tensor(snm[q][:, :], smean[q][:, :], -1.0,
                                                               srstd[q][:, :], ALU.mult, ALU.mult))
                    sVa.wait(v, tok_nm)
                    tok_xn = sVa.inc(v.tensor_scalar(xn_f[q][:, :], xt[q][:, :],
                                                     srstd[q][:, :], snm[q][:, :], ALU.mult, ALU.add))
                    tA_cmp[i] = tok_xn
                    tA_nrm[i] = tok_xn
                    sVa.wait(t, tok_xn)
                    if i >= 1:
                        sVa.wait(t, tA_ev[i - 1])
                    for c in range(NCH):
                        ins = t.transpose(pb[c // 4][:, (c % 4) * P:(c % 4 + 1) * P],
                                          xn_f[q][:, c * P:(c + 1) * P], id_s[:, :])
                    tok_tr = sTa.inc(ins)
                    sTa.wait(v, tok_tr)
                    v.tensor_copy(xnT_f[q][:, 0:4, :], pb[0][:, :])
                    tok_ev = sVa.inc(v.tensor_copy(xnT_f[q][:, 4:8, :], pb[1][:, :]))
                    tA_ev[i] = tok_ev
                    sVa.wait(sc, tok_ev)
                    sXnW[q].wait(sc)
                    tok_cst = sSa.inc(sc.copy(xnT_b[q][:, :, :], xnT_f[q][:, :, :]))
                    sSa.wait(sy, tok_cst)
                    sy.dma_start(out=xnsc.ap()[b].rearrange("c p t -> p c t")[:, :, t0:t0 + P],
                                 in_=xnT_b[q][:, :, :]).then_inc(sXnW[q].h, 16)
                    sXnW[q].n += 16
                    sVa.wait(t, tok_ev)
                    sConst.wait(t, tok_const)
                    if i >= 1:
                        sVa.wait(t, tA_rt[i - 1])
                    for c in range(NCH):
                        ins = t.matmul(pb[2][:, 0:8], xnT_f[q][:, c, :], wrout_s[:, c, :],
                                       start=(c == 0), stop=(c == NCH - 1))
                    tok_rmm = sTa.inc(ins)
                    sTa.wait(v, tok_rmm)
                    fV(lambda: v.reduce_max(rm1[:, :], pb[2][:, 0:8], AX.X))
                    fV(lambda: v.tensor_scalar(teq[:, :], pb[2][:, 0:8], rm1[:, :], None, ALU.is_ge, ALU.bypass))
                    fV(lambda: v.tensor_copy(pfa[:, 0:1], teq[:, 0:1]))
                    fV(lambda: v.tensor_tensor(pfa[:, 1:8], teq[:, 1:8], teq[:, 0:7], ALU.add))
                    fV(lambda: v.tensor_copy(pfb[:, 0:2], pfa[:, 0:2]))
                    fV(lambda: v.tensor_tensor(pfb[:, 2:8], pfa[:, 2:8], pfa[:, 0:6], ALU.add))
                    fV(lambda: v.tensor_copy(pfa[:, 0:4], pfb[:, 0:4]))
                    fV(lambda: v.tensor_tensor(pfa[:, 4:8], pfb[:, 4:8], pfb[:, 0:4], ALU.add))
                    fV(lambda: v.tensor_scalar(le8[:, :], pfa[:, :], 1.0, None, ALU.is_le, ALU.bypass))
                    fV(lambda: v.tensor_tensor(mask_s[:, i, 0:8], teq[:, :], le8[:, :], ALU.mult))
                    fV(lambda: v.scalar_tensor_tensor(msk2[:, :], mask_s[:, i, 0:8], -1e30, pb[2][:, 0:8],
                                                      ALU.mult, ALU.add))
                    fV(lambda: v.reduce_max(rm2[:, :], msk2[:, :], AX.X))
                    fV(lambda: v.tensor_scalar(teq[:, :], msk2[:, :], rm2[:, :], None, ALU.is_ge, ALU.bypass))
                    fV(lambda: v.tensor_copy(pfa[:, 0:1], teq[:, 0:1]))
                    fV(lambda: v.tensor_tensor(pfa[:, 1:8], teq[:, 1:8], teq[:, 0:7], ALU.add))
                    fV(lambda: v.tensor_copy(pfb[:, 0:2], pfa[:, 0:2]))
                    fV(lambda: v.tensor_tensor(pfb[:, 2:8], pfa[:, 2:8], pfa[:, 0:6], ALU.add))
                    fV(lambda: v.tensor_copy(pfa[:, 0:4], pfb[:, 0:4]))
                    fV(lambda: v.tensor_tensor(pfa[:, 4:8], pfb[:, 4:8], pfb[:, 0:4], ALU.add))
                    fV(lambda: v.tensor_scalar(le8[:, :], pfa[:, :], 1.0, None, ALU.is_le, ALU.bypass))
                    tok_m2 = fV(lambda: v.tensor_tensor(mask_s[:, i, 8:16], teq[:, :], le8[:, :], ALU.mult))
                    tA_rt[i] = tok_m2
                    tok_d21 = fV(lambda: v.tensor_tensor(d21[:, :], rm2[:, :], rm1[:, :], ALU.subtract))
                    sVa.wait(sc, tok_d21)
                    fS(lambda: sc.activation(e2b[:, :], d21[:, :], AF.Exp))
                    fS(lambda: sc.activation(l1b[:, :], e2b[:, :], AF.Ln, bias=1.0))
                    tok_t1 = fS(lambda: sc.activation(topv_s[:, i, 0:1], l1b[:, :], AF.Exp, scale=-1.0))
                    sSa.wait(v, tok_t1)
                    tok_t2 = fV(lambda: v.tensor_tensor(topv_s[:, i, 1:2], e2b[:, :], topv_s[:, i, 0:1], ALU.mult))
                    sVa.wait(g, tok_m2)
                    tok_tot = sGa.inc(g.tensor_reduce(tot_s[:, :], mask_s[:, i, :], AX.C, ALU.add))
                    sGa.wait(v, tok_tot)
                    if i == 0:
                        fV(lambda: v.memset(off_s[:, i, :], 0.0))
                        tok_run = fV(lambda: v.tensor_copy(run_s[:, :], tot_s[:, :]))
                    else:
                        fV(lambda: v.tensor_copy(off_s[:, i, :], run_s[:, :]))
                        tok_run = fV(lambda: v.tensor_tensor(run_s[:, :], run_s[:, :], tot_s[:, :], ALU.add))
                    tok_run_last = tok_run

                sVa.wait(g, tok_run_last)
                g.dma_start(out=ccin.ap(), in_=run_s[:, :]).then_inc(sCC.h, 16)
                sCC.n += 16
                sCC.wait(g)
                sCCC.inc(g.collective_compute("AllGather", ALU.bypass,
                                              replica_groups=[list(range(8))],
                                              ins=[ccin.ap()], outs=[ccout.ap()]), 1)
                sCCC.wait(g)
                g.dma_start(out=agb_s[:, :], in_=ccout.ap()).then_inc(sCC.h, 16)
                sCC.n += 16
                tok_agb = sCC.n
                sCC.wait(t, tok_agb)
                sConst.wait(t, tok_const)
                tok_coffmm = sTa.inc(t.matmul(pb[3][0:1, 0:16], pref_s[:, :], agb_s[:, :],
                                              start=True, stop=True))
                sTa.wait(v, tok_coffmm)
                tok_coff = sVa.inc(v.tensor_copy(coff_s[:, :], pb[3][0:1, 0:16]))

                wt8_s = sbuf(asc, "wt8_s", [NE, NTOK], F32)
                moff = sbuf(asc, "moff", [1, 16], F32)
                keep_s = sbuf(asc, "keep_s", [P, 16], F32)
                wtile = sbuf(asc, "wtile", [P, 8], F32)
                lem = sbuf(asc, "lem", [P, 16], F32)
                tok_rk_prev = None
                tok_wtr_prev = None
                sSa.wait(v)
                for j in range(NTILE):
                    sVa.wait(v, tok_coff)
                    tok_moff = sVa.inc(v.tensor_tensor(moff[:, :], off_s[:, j, :], coff_s[:, :], ALU.add))
                    sVa.wait(t, tok_moff)
                    if tok_rk_prev is not None:
                        sVa.wait(t, tok_rk_prev)
                    t.matmul(pb[3][:, 0:16], triu_s[:, :], mask_s[:, j, :], start=True, stop=False)
                    tok_rk = sTa.inc(t.matmul(pb[3][:, 0:16], ones_s[:, :], moff[:, :],
                                              start=False, stop=True))
                    sTa.wait(v, tok_rk)
                    fV(lambda: v.tensor_scalar(lem[:, :], pb[3][:, 0:16], CAP, None, ALU.is_le, ALU.bypass))
                    tok_keep = fV(lambda: v.tensor_tensor(keep_s[:, :], lem[:, :], mask_s[:, j, :], ALU.mult))
                    tok_rk_prev = tok_keep
                    fV(lambda: v.tensor_scalar(wtile[:, :], keep_s[:, 0:8], topv_s[:, j, 0:1], None, ALU.mult, ALU.bypass))
                    tok_w = fV(lambda: v.scalar_tensor_tensor(wtile[:, :], keep_s[:, 8:16],
                                                              topv_s[:, j, 1:2], wtile[:, :], ALU.mult, ALU.add))
                    sVa.wait(t, tok_w)
                    if tok_wtr_prev is not None:
                        sVa.wait(t, tok_wtr_prev)
                    tok_wtr = sTa.inc(t.transpose(pb[4][0:8, 0:P], wtile[:, :], id_s[:, :]))
                    sTa.wait(v, tok_wtr)
                    tok_wev = sVa.inc(v.tensor_copy(wt8_s[:, j * P:(j + 1) * P], pb[4][0:8, 0:P]))
                    tok_wtr_prev = tok_wev
                sVa.wait(sy, tok_wev)
                sy.dma_start(out=wt8sc.ap(), in_=wt8_s[:, :]).then_inc(sWT8.h, 16)
                sWT8.n += 16
                asc.close()
                sTa.wait(sy)
                sVa.wait(sy)
                sSa.wait(sy)
                sGa.wait(sy)
                sWT8.wait(sy)

                if "B" in PHASES:
                    xw = [sbuf(ab, "xw0", [P, NCH, TCB + 1], BF16),
                          sbuf(ab, "xw1", [P, NCH, TCB + 1], BF16)]
                    xa_t = sbuf(ab, "xa_t", [P, TCB], BF16)
                    mk_b = sbuf(ab, "mk_b", [P, NCH, TCB], BF16)
                    mv_b = sbuf(ab, "mv_b", [P, NCH, TCB], BF16)
                    mr_b = sbuf(ab, "mr_b", [P, NCH, TCB], BF16)
                    ek_w = sbuf(ab, "ek_w", [P, NCH, TCB + 1], F32)
                    ekv_w = sbuf(ab, "ekv_w", [P, NCH, TCB + 1], F32)
                    aex_b = sbuf(ab, "aex_b", [P, NCH, TCB], F32)
                    bex_b = sbuf(ab, "bex_b", [P, NCH, TCB], F32)
                    sra_b = sbuf(ab, "sra_b", [P, NCH, TCB], F32)
                    rw_b = sbuf(ab, "rw_b", [P, NCH, TCB], BF16)
                    sev_s = sbuf(ab, "sev_s", [P, NCH, TCB], F32)
                    carA = sbuf(ab, "carA", [P, NCH], F32)
                    carB = sbuf(ab, "carB", [P, NCH], F32)
                    ones_tc = sbuf(ab, "ones_tc", [P, TCB], F32)
                    jnk_s = sbuf(ab, "jnk_s", [P, 1], F32)
                    numb_b = sbuf(ab, "numb_b", [P, NCH, TCB], F32)
                    denb_b = sbuf(ab, "denb_b", [P, NCH, TCB], F32)
                    swa = sbuf(ab, "swa", [P, NCH, 2 * TCB], F32)
                    swb = sbuf(ab, "swb", [P, NCH, 2 * TCB], F32)
                    ewpow = sbuf(ab, "ewpow", [P, NCH, TCB], F32)
                    ew2k = sbuf(ab, "ew2k", [P, NCH, 8], F32)

                    sWT8.wait(v)
                    sXnW[0].wait(v)
                    sXnW[1].wait(v)
                    sSa.wait(v, tok_ewu)
                    v.memset(ones_tc[:, :], 1.0)
                    fB = lambda emit: (sVb.wait(v), sVb.inc(emit()))[1]
                    fB(lambda: v.tensor_copy(ew2k[:, :, 0], ew_s[:, :]))
                    for k in range(1, 8):
                        fB(lambda k=k: v.tensor_tensor(ew2k[:, :, k:k + 1], ew2k[:, :, k - 1:k],
                                                       ew2k[:, :, k - 1:k], ALU.mult))
                    fB(lambda: v.memset(swa[:, :, :], 1.0))
                    fB(lambda: v.memset(swb[:, :, :], 1.0))
                    for c in range(NCH):
                        fB(lambda c=c: v.tensor_scalar(swa[:, c, TCB:2 * TCB], ones_tc[:, :],
                                                       ew_s[:, c:c + 1], None, ALU.mult, ALU.bypass))
                    cur, nxt = swa, swb
                    for k in range(8):
                        sh = 1 << k
                        fB(lambda cur=cur, nxt=nxt, sh=sh: v.tensor_tensor(
                            nxt[:, :, TCB:2 * TCB], cur[:, :, TCB - sh:2 * TCB - sh],
                            cur[:, :, TCB:2 * TCB], ALU.mult))
                        cur, nxt = nxt, cur
                    tok_ewb = fB(lambda cur=cur: v.tensor_copy(ewpow[:, :, :], cur[:, :, TCB:2 * TCB]))
                    fB(lambda: v.memset(swa[:, :, 0:TCB], 0.0))
                    tok_pad = fB(lambda: v.memset(swb[:, :, 0:TCB], 0.0))

                    tB_mix_done = {}
                    tB_ek_last = 0
                    tB_ekv_last = 0
                    tB_sr_last = 0
                    for it in range(BC * NB_B):
                        b, j = it // NB_B, it % NB_B
                        t0 = j * TCB
                        q = it % 2
                        if B_SERIAL:
                            for eng in (sy, t, v, sc):
                                sVb.wait(eng)
                                sSb.wait(eng)
                                sTb.wait(eng)
                        if it >= 2:
                            sVb.wait(sy, tB_mix_done[q])
                        sXnW[0].wait(sy)
                        sXnW[1].wait(sy)
                        if j == 0:
                            sy.dma_start(out=xw[q][:, :, 1:TCB + 1],
                                         in_=xnsc.ap()[b].rearrange("c p t -> p c t")[:, :, 0:TCB]
                                         ).then_inc(sXW[q].h, 16)
                        else:
                            sy.dma_start(out=xw[q][:, :, 0:TCB + 1],
                                         in_=xnsc.ap()[b].rearrange("c p t -> p c t")[:, :, t0 - 1:t0 + TCB]
                                         ).then_inc(sXW[q].h, 16)
                        sXW[q].n += 16
                        tok_xw = sXW[q].n
                        sXW[q].wait(v, tok_xw)
                        if j == 0:
                            v.memset(xw[q][:, :, 0:1], 0.0)
                        for (mm_b, tm, tm1) in ((mk_b, tmk_s, tmk1_s), (mv_b, tmv_s, tmv1_s),
                                                (mr_b, tmr_s, tmr1_s)):
                            for c in range(NCH):
                                v.tensor_scalar(xa_t[:, :], xw[q][:, c, 1:TCB + 1], tm[:, c:c + 1],
                                                None, ALU.mult, ALU.bypass)
                                ins = v.scalar_tensor_tensor(mm_b[:, c, :], xw[q][:, c, 0:TCB],
                                                             tm1[:, c:c + 1], xa_t[:, :], ALU.mult, ALU.add)
                        tok_mix = sVb.inc(ins)
                        tB_mix_done[q] = tok_mix
                        sVb.wait(v, tok_mix)
                        tok_mixf = sVb.inc(v.tensor_copy(jnk_s[:, :], mr_b[:, NCH - 1, TCB - 1:TCB]))
                        sVb.wait(t, tok_mixf)
                        sPrepR.wait(t, tok_rwkv_w)
                        tok_kmm = [None] * NCH
                        tok_vmm = [None] * NCH
                        tok_rmm_ = [None] * (NCH + 1)
                        tok_ek = [None] * NCH
                        tok_ekv = [None] * NCH
                        tok_sr1 = [None] * NCH
                        for mc in range(NCH):
                            bk = pb[mc % 2]
                            if it > 0 or mc >= 2:
                                sSb.wait(t, tok_ek[mc - 2] if mc >= 2 else tB_ek_last)
                            for kc in range(NCH):
                                ins = t.matmul(bk[:, 0:TCB], wk_b[:, kc, mc * P:(mc + 1) * P],
                                               mk_b[:, kc, :], start=(kc == 0), stop=(kc == NCH - 1))
                            tok_kmm[mc] = sTb.dinc(t)
                            if mc >= 1:
                                sTb.wait(sc, tok_kmm[mc])
                                tok_ek[mc - 1] = sSb.inc(sc.activation(ek_w[:, mc - 1, 1:TCB + 1],
                                                                       pb[(mc - 1) % 2][:, 0:TCB], AF.Exp))
                        for mc in range(NCH):
                            bk = pb[2 + mc % 2]
                            if it > 0 or mc >= 2:
                                sVb.wait(t, tok_ekv[mc - 2] if mc >= 2 else tB_ekv_last)
                            for kc in range(NCH):
                                ins = t.matmul(bk[:, 0:TCB], wv_b[:, kc, mc * P:(mc + 1) * P],
                                               mv_b[:, kc, :], start=(kc == 0), stop=(kc == NCH - 1))
                            tok_vmm[mc] = sTb.dinc(t)
                            if mc == 0:
                                sTb.wait(sc, tok_vmm[0])
                                tok_ek[NCH - 1] = sSb.inc(sc.activation(ek_w[:, NCH - 1, 1:TCB + 1],
                                                                        pb[(NCH - 1) % 2][:, 0:TCB], AF.Exp))
                            else:
                                sTb.wait(v, tok_vmm[mc])
                                sSb.wait(v, tok_ek[mc - 1])
                                tok_ekv[mc - 1] = sVb.inc(v.tensor_tensor(ekv_w[:, mc - 1, 1:TCB + 1],
                                                                          ek_w[:, mc - 1, 1:TCB + 1],
                                                                          pb[2 + (mc - 1) % 2][:, 0:TCB],
                                                                          ALU.mult))
                        tB_ek_last = tok_ek[NCH - 1]
                        for mc in range(NCH + 1):
                            if mc < NCH:
                                bk = pb[4 + mc % 2]
                                if it > 0 or mc >= 2:
                                    sSb.wait(t, tok_sr1[mc - 2] if mc >= 2 else tB_sr_last)
                                for kc in range(NCH):
                                    ins = t.matmul(bk[:, 0:TCB], wr_b[:, kc, mc * P:(mc + 1) * P],
                                                   mr_b[:, kc, :], start=(kc == 0), stop=(kc == NCH - 1))
                                tok_rmm_[mc] = sTb.dinc(t)
                            else:
                                sSb.wait(t, tok_ek[NCH - 2])
                                ins = t.matmul(pb[0][:, 0:TCB], wr_b[:, 0, 0:P], mr_b[:, 0, :],
                                               start=True, stop=True)
                                tok_rmm_[mc] = sTb.dinc(t)
                            if mc == 0:
                                sTb.wait(v, tok_rmm_[0])
                                sSb.wait(v, tok_ek[NCH - 1])
                                tok_ekv[NCH - 1] = sVb.inc(v.tensor_tensor(ekv_w[:, NCH - 1, 1:TCB + 1],
                                                                           ek_w[:, NCH - 1, 1:TCB + 1],
                                                                           pb[2 + (NCH - 1) % 2][:, 0:TCB],
                                                                           ALU.mult))
                            else:
                                sTb.wait(sc, tok_rmm_[mc])
                                tok_sr1[mc - 1] = sSb.inc(sc.activation(sra_b[:, mc - 1, :],
                                                                        pb[4 + (mc - 1) % 2][:, 0:TCB],
                                                                        AF.Exp, scale=-1.0))
                        tB_ekv_last = tok_ekv[NCH - 1]
                        tB_sr_last = tok_sr1[NCH - 1]
                        sSb.wait(sc)
                        sSb.inc(sc.activation(denb_b[:, :, :], sra_b[:, :, :], AF.Ln, bias=1.0))
                        sSb.wait(sc)
                        tok_sr = sSb.inc(sc.activation(sra_b[:, :, :], denb_b[:, :, :], AF.Exp, scale=-1.0))
                        scan_wait = tok_ekv[NCH - 1]
                        if j == 0:
                            v.memset(ek_w[:, :, 0:1], 0.0)
                            v.memset(ekv_w[:, :, 0:1], 0.0)
                            v.memset(carA[:, :], 0.0)
                            tok_bnd = sVb.inc(v.memset(carB[:, :], 0.0))
                            scan_wait = tok_bnd
                        sVb.wait(v, scan_wait)
                        for (srcw, dst, car) in ((ekv_w, aex_b, carA), (ek_w, bex_b, carB)):
                            sVb.wait(v)
                            sVb.inc(v.tensor_copy(swa[:, :, TCB:2 * TCB], srcw[:, :, 0:TCB]))
                            cur, nxt = swa, swb
                            for k in range(8):
                                sh = 1 << k
                                sVb.wait(v)
                                for c in range(NCH):
                                    ins = v.scalar_tensor_tensor(
                                        nxt[:, c, TCB:2 * TCB], cur[:, c, TCB - sh:2 * TCB - sh],
                                        ew2k[:, c, k:k + 1], cur[:, c, TCB:2 * TCB],
                                        ALU.mult, ALU.add)
                                sVb.inc(ins)
                                cur, nxt = nxt, cur
                            sVb.wait(v)
                            for c in range(NCH):
                                ins = v.scalar_tensor_tensor(
                                    dst[:, c, :], ewpow[:, c, :], car[:, c:c + 1],
                                    cur[:, c, TCB:2 * TCB], ALU.mult, ALU.add)
                            sVb.inc(ins)
                        tok_scan = sVb.inc(v.tensor_copy(jnk_s[:, :], bex_b[:, NCH - 1, TCB - 1:TCB]))
                        if DEBUG_B3:
                            sVb.wait(sy, tok_scan)
                            sy.dma_start(out=dbg["ekvall"].ap()[it], in_=ekv_w[:, :, :]).then_inc(sDbgB.h, 16)
                            sDbgB.n += 16
                            sy.dma_start(out=dbg["ekall"].ap()[it], in_=ek_w[:, :, :]).then_inc(sDbgB.h, 16)
                            sDbgB.n += 16
                            sy.dma_start(out=dbg["aexall"].ap()[it], in_=aex_b[:, :, :]).then_inc(sDbgB.h, 16)
                            sDbgB.n += 16
                            sy.dma_start(out=dbg["bexall"].ap()[it], in_=bex_b[:, :, :]).then_inc(sDbgB.h, 16)
                            sDbgB.n += 16
                        sVb.wait(v, tok_scan)
                        v.tensor_copy(carA[:, :], aex_b[:, :, TCB - 1])
                        v.tensor_copy(carB[:, :], bex_b[:, :, TCB - 1])
                        v.tensor_copy(ek_w[:, :, 0:1], ek_w[:, :, TCB:TCB + 1])
                        tok_car = sVb.inc(v.tensor_copy(ekv_w[:, :, 0:1], ekv_w[:, :, TCB:TCB + 1]))
                        if DEBUG_B and it == IT_DUMP:
                            sVb.wait(g, tok_car)
                            g.dma_start(out=dbg["aex"].ap(), in_=aex_b[:, :, :]).then_inc(sDbgB.h, 16)
                            sDbgB.n += 16
                            g.dma_start(out=dbg["bex"].ap(), in_=bex_b[:, :, :]).then_inc(sDbgB.h, 16)
                            sDbgB.n += 16
                            sDbgB.wait(v)
                        for c in range(NCH):
                            v.scalar_tensor_tensor(numb_b[:, c, :], ekv_w[:, c, 1:TCB + 1],
                                                   eu_s[:, c:c + 1], aex_b[:, c, :], ALU.mult, ALU.add)
                            ins = v.scalar_tensor_tensor(denb_b[:, c, :], ek_w[:, c, 1:TCB + 1],
                                                         eu_s[:, c:c + 1], bex_b[:, c, :], ALU.mult, ALU.add)
                        tok_numden = sVb.inc(ins)
                        sVb.wait(v, tok_numden)
                        tok_rden = sVb.inc(v.reciprocal(swb[:, :, TCB:2 * TCB], denb_b[:, :, :]))
                        sVb.wait(v, tok_rden)
                        v.tensor_tensor(aex_b[:, :, :], numb_b[:, :, :], swb[:, :, TCB:2 * TCB], ALU.mult)
                        sSb.wait(v, tok_sr)
                        sVb.inc(v.tensor_tensor(rw_b[:, :, :], aex_b[:, :, :], sra_b[:, :, :],
                                                ALU.mult))
                        sVb.wait(v)
                        tok_rw = sVb.inc(v.tensor_copy(jnk_s[:, :], rw_b[:, NCH - 1, TCB - 1:TCB]))
                        if DEBUG_B3:
                            sVb.wait(sy, tok_rw)
                            sy.dma_start(out=dbg["rwall"].ap()[it], in_=rw_b[:, :, :]).then_inc(sDbgB.h, 16)
                            sDbgB.n += 16
                            sy.dma_start(out=dbg["wkvall"].ap()[it], in_=aex_b[:, :, :]).then_inc(sDbgB.h, 16)
                            sDbgB.n += 16
                            sy.dma_start(out=dbg["sraall"].ap()[it], in_=sra_b[:, :, :]).then_inc(sDbgB.h, 16)
                            sDbgB.n += 16
                        sVb.wait(t, tok_rw)
                        if it >= 1:
                            sSw[0].wait(t)
                        tok_omm = [None] * (NCH + 1)
                        tok_oev = [None] * NCH
                        for mc in range(NCH + 1):
                            if mc < NCH:
                                bk = pb[6 + mc % 2]
                                if mc >= 2:
                                    sVb.wait(t, tok_oev[mc - 2])
                                for kc in range(NCH):
                                    ins = t.matmul(bk[:, 0:TCB], wo_b[:, kc, mc * P:(mc + 1) * P],
                                                   rw_b[:, kc, :], start=(kc == 0), stop=(kc == NCH - 1))
                                tok_omm[mc] = sTb.dinc(t)
                            else:
                                ins = t.matmul(pb[5][:, 0:TCB], wo_b[:, 0, 0:P], rw_b[:, 0, :],
                                               start=True, stop=True)
                                tok_omm[mc] = sTb.dinc(t)
                            if mc >= 1:
                                sTb.wait(v, tok_omm[mc])
                                tok_oev[mc - 1] = sVb.inc(v.tensor_copy(sev_s[:, mc - 1, :],
                                                                        pb[6 + (mc - 1) % 2][:, 0:TCB]))
                        sVb.wait(sy, tok_oev[NCH - 1])
                        sy.dma_start(out=ssc.ap()[b].rearrange("c p t -> p c t")[:, :, t0:t0 + TCB],
                                     in_=sev_s[:, :, :]).then_inc(sSw[0].h, 16)
                        sSw[0].n += 16
                        if DEBUG_B and it == IT_DUMP:
                            sVb.wait(g)
                            sSb.wait(g)
                            sTb.wait(g)
                            for nm, buf in [("mk", mk_b), ("ek", ek_w), ("ekv", ekv_w),
                                            ("den", den_b), ("sra", sra_b), ("rw", rw_b),
                                            ("sev", sev_s), ("carA", carA), ("carB", carB)]:
                                ap_src = buf[:, :] if nm in ("carA", "carB") else buf[:, :, :]
                                g.dma_start(out=dbg[nm].ap(), in_=ap_src).then_inc(sDbgB.h, 16)
                                sDbgB.n += 16
                            sDbgB.wait(v)
                            sDbgB.wait(sc)
                            sDbgB.wait(t)

            if "C" in PHASES:
                with ExitStack() as cs:
                    xnblk = sbuf(cs, "xnblk", [P, NCH, BLK], BF16)
                    h_sb = sbuf(cs, "h_sb", [P, FF // P, BLK], BF16)
                    acc = sbuf(cs, "acc", [P, NCH, BLK], F32)
                    w1ch = [sbuf(cs, "w1ch0", [P, NCH, 512], BF16),
                            sbuf(cs, "w1ch1", [P, NCH, 512], BF16)]
                    w2ch = [sbuf(cs, "w2ch0", [P, 4, D], BF16),
                            sbuf(cs, "w2ch1", [P, 4, D], BF16)]
                    wb_sb = sbuf(cs, "wb_sb", [P, 2, 512], F32)
                    wrow = sbuf(cs, "wrow", [1, 2, BLK], F32)
                    wt8b = sbuf(cs, "wt8b", [NE, 2, 512], F32)
                    ytmp = sbuf(cs, "ytmp", [P, 512], F32)

                    for eng in (sy, t, v, sc):
                        sVb.wait(eng)
                        sSb.wait(eng)
                        sTb.wait(eng)
                        sSw[0].wait(eng)
                        sSw[1].wait(eng)
                        sVa.wait(eng)
                        sWT8.wait(eng)
                    sPrep.wait(sy)
                    if DEBUG and "B" in PHASES:
                        sSw[0].wait(g)
                        sSw[1].wait(g)
                        g.dma_start(out=dbg["s2"].ap(), in_=ssc.ap()).then_inc(sDbgB.h, 16)
                        sDbgB.n += 16

                    tok_yev = {}
                    tok_yev01 = {}
                    tok_gelu = {}
                    tok_b2cp = None
                    tok_w1 = {}
                    tok_w2 = {}
                    tok_mm1g = {}
                    tok_w2mm = {}
                    tok_wbmm_by_e = {}

                    for blk in range(NBLK):
                        b, t0 = blk // (NBLK // BC), (blk % (NBLK // BC)) * BLK
                        gtok0 = blk * BLK
                        if blk > 0:
                            sTc.wait(sy, tok_mm1_last)
                            sMw.wait(sy, 16 * blk)
                        sy.dma_start(out=xnblk[:, :, :],
                                     in_=xnsc.ap()[b].rearrange("c p t -> p c t")[:, :, t0:t0 + BLK]
                                     ).then_inc(sXnB.h, 16)
                        sXnB.n += 16
                        tok_xnb = sXnB.n
                        sy.dma_start(out=wt8b[:, :, :],
                                     in_=wt8sc.ap()[:, gtok0:gtok0 + BLK].rearrange("e (n f) -> e n f", n=2)
                                     ).then_inc(sWt8b.h, 16)
                        sWt8b.n += 16
                        tok_wt8b = sWt8b.n
                        sWt8b.wait(t, tok_wt8b)
                        sConst.wait(t, tok_const)
                        if blk > 0:
                            sVc.wait(t, tok_b2_free)
                        tok_b2mm = {}
                        for nh in range(2):
                            for mcd in range(NCH):
                                ins = t.matmul(pb[7][:, 0:512], b2_s[:, mcd * P:(mcd + 1) * P],
                                               wt8b[:, nh, :], start=True, stop=True)
                                tok_b2mm[(nh, mcd)] = sTc.inc(ins)
                                sTc.wait(v, tok_b2mm[(nh, mcd)])
                                if tok_b2cp is not None:
                                    sMw.wait(v, 16 * blk)
                                ins2 = v.tensor_copy(acc[:, mcd, nh * 512:(nh + 1) * 512], pb[7][:, 0:512])
                                tok_b2cp = sVc.inc(ins2)
                                sVc.wait(t, tok_b2cp)
                        tok_b2_free = tok_b2cp

                        for e in range(NE):
                            qe = e % 2
                            if e >= 2 or blk > 0:
                                sTc.wait(sy, tok_wbmm_by_e[(e - 2) % NE])
                            sy.dma_start(out=wrow[0:1, qe, :],
                                         in_=wt8sc.ap()[e:e + 1, gtok0:gtok0 + BLK]).then_inc(sWrow.h, 16)
                            sWrow.n += 16
                            tok_wrow = sWrow.n
                            gidx = 0
                            for fc in range(NCH):
                                qw = fc % 2
                                key = (blk, e, fc)
                                prev = (blk, e, fc - 2) if fc >= 2 else ((blk, e - 1, NCH - 2 + fc) if e > 0 else ((blk - 1, NE - 1, NCH - 2 + fc) if blk > 0 else None))
                                if prev in tok_mm1g:
                                    sTc.wait(sy, tok_mm1g[prev])
                                sy.dma_start(out=w1ch[qw][:, :, :],
                                             in_=w1sc.ap()[e].rearrange("(kc p) f -> p kc f", p=P)
                                             [:, :, fc * 512:(fc + 1) * 512]).then_inc(sW1c[qw].h, 16)
                                sW1c[qw].n += 16
                                tok_w1[key] = sW1c[qw].n
                                sW1c[qw].wait(t, tok_w1[key])
                                sXnB.wait(t, tok_xnb)
                                for mc in range(4):
                                    for nh in range(2):
                                        hb = pb[gidx % 2]
                                        gk = (blk, e, gidx)
                                        gprev = (blk, e, gidx - 2)
                                        if gidx >= 2 and gprev in tok_gelu:
                                            sGc.wait(t, tok_gelu[gprev])
                                        elif gidx < 2 and (blk, e) != (0, 0):
                                            sVc.wait(t, tok_yev01.get((blk, e - 1, 1) if e > 0 else (blk - 1, NE - 1, 1), 0))
                                            sGc.wait(t)
                                        for kc in range(NCH):
                                            ins = t.matmul(hb[:, 0:512],
                                                           w1ch[qw][:, kc, mc * P:(mc + 1) * P],
                                                           xnblk[:, kc, nh * 512:(nh + 1) * 512],
                                                           start=(kc == 0), stop=(kc == NCH - 1))
                                        tok_mm1g[key] = sTc.inc(ins)
                                        sTc.wait(sc, tok_mm1g[key])
                                        gins = sc.activation(h_sb[:, fc * 4 + mc, nh * 512:(nh + 1) * 512],
                                                             hb[:, 0:512], AF.Gelu,
                                                             bias=b1_s[:, e, fc * 4 + mc:fc * 4 + mc + 1])
                                        tok_gelu[gk] = sGc.inc(gins)
                                        gidx += 1
                            tok_mm1_last = tok_mm1g[(blk, e, NCH - 1)]
                            tok_gelu[(blk, e)] = tok_gelu[(blk, e, 63)]
                            sWrow.wait(t, tok_wrow)
                            if (blk, e) != (0, 0):
                                pk = (blk, e - 1, 1) if e > 0 else (blk - 1, NE - 1, 1)
                                sVc.wait(t, tok_yev.get(pk, 0))
                            for nh in range(2):
                                ins = t.matmul(pb[6][:, 0:512] if nh == 0 else pb[5][:, 0:512],
                                               ones_s[:, :], wrow[0:1, qe, nh * 512:(nh + 1) * 512],
                                               start=True, stop=True)
                            tok_wbmm = sTc.inc(ins)
                            tok_wbmm_by_e[e] = tok_wbmm
                            sTc.wait(v, tok_wbmm)
                            v.tensor_copy(wb_sb[:, 0, :], pb[6][:, 0:512])
                            tok_wb = sVc.inc(v.tensor_copy(wb_sb[:, 1, :], pb[5][:, 0:512]))
                            for nh in range(2):
                                sGc.wait(t, tok_gelu[(blk, e)])
                                sVc.wait(t, tok_wb)
                                pk = (blk, e, nh - 1) if nh > 0 else ((blk, e - 1, 1) if e > 0 else ((blk - 1, NE - 1, 1) if blk > 0 else None))
                                if pk in tok_yev:
                                    sVc.wait(t, tok_yev[pk])
                                for fh in range(NCH):
                                    qw = fh % 2
                                    keyw = (blk, e, nh, fh)
                                    prevw = (blk, e, nh, fh - 2) if fh >= 2 else ((blk, e, nh - 1, NCH - 2 + fh) if nh > 0 else ((blk, e - 1, 1, NCH - 2 + fh) if e > 0 else ((blk - 1, NE - 1, 1, NCH - 2 + fh) if blk > 0 else None)))
                                    if prevw in tok_w2mm:
                                        sTc.wait(sy, tok_w2mm[prevw])
                                    sy.dma_start(out=w2ch[qw][:, :, :],
                                                 in_=w2sc.ap()[e][fh * 512:(fh + 1) * 512, :]
                                                 .rearrange("(a p) d -> p a d", p=P)).then_inc(sW2c[qw].h, 16)
                                    sW2c[qw].n += 16
                                    tok_w2[keyw] = sW2c[qw].n
                                    sW2c[qw].wait(t, tok_w2[keyw])
                                    for ka in range(4):
                                        kf = fh * 4 + ka
                                        for dh in range(2):
                                            for mc in range(4):
                                                ins = t.matmul(pb[dh * 4 + mc][:, 0:512],
                                                               w2ch[qw][:, ka, dh * 512 + mc * P:dh * 512 + (mc + 1) * P],
                                                               h_sb[:, kf, nh * 512:(nh + 1) * 512],
                                                               start=(kf == 0), stop=(kf == FF // P - 1))
                                    tok_w2mm[keyw] = sTc.inc(ins)
                                tok_mm2 = tok_w2mm[(blk, e, nh, NCH - 1)]
                                sTc.wait(v, tok_mm2)
                                sVc.wait(v, tok_wb)
                                first_tok = None
                                for dh in range(2):
                                    for mc in range(4):
                                        v.tensor_tensor(ytmp[:, :], pb[dh * 4 + mc][:, 0:512],
                                                        wb_sb[:, nh, :], ALU.mult)
                                        ins = v.tensor_tensor(acc[:, dh * 4 + mc, nh * 512:(nh + 1) * 512],
                                                              acc[:, dh * 4 + mc, nh * 512:(nh + 1) * 512],
                                                              ytmp[:, :], ALU.add)
                                        if dh == 0 and mc == 1:
                                            first_tok = sVc.inc(ins)
                                tok_ye = sVc.inc(ins)
                                tok_yev[(blk, e, nh)] = tok_ye
                                tok_yev01[(blk, e, nh)] = first_tok
                        sVc.wait(sy, tok_yev[(blk, NE - 1, 1)])
                        sy.dma_start(out=moesc.ap()[blk].rearrange("c p t -> p c t"),
                                     in_=acc[:, :, :]).then_inc(sMw.h, 16)
                        sMw.n += 16

            if "D" in PHASES:
                with ExitStack() as ds:
                    s_blk = sbuf(ds, "s_blk", [P, NCH, BLK], F32)
                    m_blk = sbuf(ds, "m_blk", [P, NCH, BLK], F32)
                    g_blk = sbuf(ds, "g_blk", [P, NCH, BLK], F32)
                    x_blk = sbuf(ds, "x_blk", [P, NCH, D], F32)
                    o_sb = [sbuf(ds, "o_sb0", [P, D], F32), sbuf(ds, "o_sb1", [P, D], F32)]

                    for eng in (sy, t, v, sc):
                        sVc.wait(eng)
                        sGc.wait(eng)
                        sTc.wait(eng)
                        sMw.wait(eng)

                    tok_osb = [0, 0]
                    tok_ost = [0, 0]
                    for blk in range(NBLK):
                        b, t0 = blk // (NBLK // BC), (blk % (NBLK // BC)) * BLK
                        if blk > 0:
                            sTd.wait(sy, tok_tr_all)
                            sSd.wait(sy, tok_g)
                            sVd.wait(sy)
                        sSw[0].wait(sy)
                        sSw[1].wait(sy)
                        sMw.wait(sy)
                        sy.dma_start(out=s_blk[:, :, :],
                                     in_=ssc.ap()[b].rearrange("c p t -> p c t")[:, :, t0:t0 + BLK]
                                     ).then_inc(sLd.h, 16)
                        sLd.n += 16
                        sy.dma_start(out=m_blk[:, :, :],
                                     in_=moesc.ap()[blk].rearrange("c p t -> p c t")).then_inc(sLd.h, 16)
                        sLd.n += 16
                        sy.dma_start(out=x_blk[:, :, :],
                                     in_=x_in.ap()[b, t0:t0 + BLK, :].rearrange("(s p) d -> p s d", p=P)
                                     ).then_inc(sLd.h, 16)
                        sLd.n += 16
                        tok_ld = sLd.n
                        sLd.wait(sc, tok_ld)
                        if blk > 0:
                            sVd.wait(sc, tok_p)
                        tok_g = sSd.inc(sc.activation(g_blk[:, :, :], m_blk[:, :, :], AF.Silu))
                        sSd.wait(v, tok_g)
                        sLd.wait(v, tok_ld)
                        tok_p = sVd.inc(v.tensor_tensor(s_blk[:, :, :], s_blk[:, :, :], g_blk[:, :, :],
                                                        ALU.mult))
                        for sidx in range(NCH):
                            qo = sidx % 2
                            sVd.wait(t, tok_p)
                            if blk > 0 or sidx >= 2:
                                sVd.wait(t, tok_osb[qo])
                            for c in range(NCH):
                                ins = t.transpose(pb[2 * qo + c // 4][:, (c % 4) * P:(c % 4 + 1) * P],
                                                  s_blk[:, c, sidx * P:(sidx + 1) * P], id_s[:, :])
                            tok_tr = sTd.inc(ins)
                            sTd.wait(v, tok_tr)
                            if blk > 0 or sidx >= 2:
                                sOut[qo].wait(v, tok_ost[qo])
                            v.tensor_tensor(o_sb[qo][:, 0:512], pb[2 * qo][:, :],
                                            x_blk[:, sidx, 0:512], ALU.add)
                            ins = v.tensor_tensor(o_sb[qo][:, 512:1024], pb[2 * qo + 1][:, :],
                                                  x_blk[:, sidx, 512:1024], ALU.add)
                            tok_osb[qo] = sVd.inc(ins)
                            sVd.wait(sy, tok_osb[qo])
                            sy.dma_start(out=out.ap()[b, t0 + sidx * P:t0 + (sidx + 1) * P, :],
                                         in_=o_sb[qo][:, :]).then_inc(sOut[qo].h, 16)
                            sOut[qo].n += 16
                            tok_ost[qo] = sOut[qo].n
                        tok_tr_all = tok_tr

            if DEBUG:
                sXnW[0].wait(sy)
                sXnW[1].wait(sy)
                sWT8.wait(sy)
                for eng2 in (t, v, sc):
                    sVa.wait(eng2)
                sy.dma_start(out=dbg["xn"].ap(), in_=xnsc.ap()).then_inc(sOut[0].h, 16)
                sOut[0].n += 16
                if "B" in PHASES:
                    sSw[0].wait(sy)
                    sSw[1].wait(sy)
                    sy.dma_start(out=dbg["s"].ap(), in_=ssc.ap()).then_inc(sOut[0].h, 16)
                    sOut[0].n += 16
                if "C" in PHASES:
                    sMw.wait(sy)
                    sy.dma_start(out=dbg["moe"].ap(), in_=moesc.ap()).then_inc(sOut[0].h, 16)
                    sOut[0].n += 16
                sy.dma_start(out=dbg["wt8"].ap(), in_=wt8sc.ap()).then_inc(sOut[0].h, 16)
                sOut[0].n += 16
            sDbgB.wait(sy)
            sOut[0].wait(sy)
            sOut[1].wait(sy)

    return nc


_NC_CACHE = {}

def _get_nc():
    key = (DEBUG, PHASES)
    if key not in _NC_CACHE:
        _NC_CACHE[key] = build_nc()
    return _NC_CACHE[key]


def kernel(L0_attn_x, L0_attn_time_decay, L0_attn_time_first, L0_attn_time_mix_k,
           L0_attn_time_mix_v, L0_attn_time_mix_r, L0_attn_Wk, L0_attn_Wv,
           L0_attn_Wr, L0_attn_Wo, L0_ffn_Wr, L0_ffn_W1, L0_ffn_b1, L0_ffn_W2,
           L0_ffn_b2, _return_raw=False):
    f32 = np.float32
    x = np.ascontiguousarray(np.asarray(L0_attn_x, dtype=f32))
    wkT = np.ascontiguousarray(np.asarray(L0_attn_Wk, f32).T)
    wvT = np.ascontiguousarray(np.asarray(L0_attn_Wv, f32).T)
    wrT = np.ascontiguousarray(np.asarray(L0_attn_Wr, f32).T)
    woT = np.ascontiguousarray(np.asarray(L0_attn_Wo, f32).T)
    wrout = np.ascontiguousarray(np.asarray(L0_ffn_Wr, f32))
    w1 = np.ascontiguousarray(np.asarray(L0_ffn_W1, f32))
    w2 = np.ascontiguousarray(np.asarray(L0_ffn_W2, f32))
    b1T = np.ascontiguousarray(np.asarray(L0_ffn_b1, f32).reshape(NE, FF // P, P).transpose(0, 2, 1))
    b2 = np.ascontiguousarray(np.asarray(L0_ffn_b2, f32))
    tdT = np.ascontiguousarray(np.asarray(L0_attn_time_decay, f32).reshape(NCH, P).T)
    tfT = np.ascontiguousarray(np.asarray(L0_attn_time_first, f32).reshape(NCH, P).T)
    tmkT = np.ascontiguousarray(np.asarray(L0_attn_time_mix_k, f32).reshape(NCH, P).T)
    tmvT = np.ascontiguousarray(np.asarray(L0_attn_time_mix_v, f32).reshape(NCH, P).T)
    tmrT = np.ascontiguousarray(np.asarray(L0_attn_time_mix_r, f32).reshape(NCH, P).T)
    ident = np.eye(P, dtype=f32)
    triu = np.triu(np.ones((P, P), dtype=f32))
    onesr = np.ones((1, P), dtype=f32)

    nc = _get_nc()
    in_maps = []
    for c in range(8):
        in_maps.append(dict(
            x=np.ascontiguousarray(x[2 * c:2 * c + 2]),
            wkT=wkT, wvT=wvT, wrT=wrT, woT=woT, wrout=wrout,
            w1=w1, w2=w2, b1T=b1T, b2=b2, tdT=tdT, tfT=tfT,
            tmkT=tmkT, tmvT=tmvT, tmrT=tmrT,
            ident=ident, triu=triu, onesr=onesr,
            prefc=np.ascontiguousarray((np.arange(NE) < c).astype(f32).reshape(NE, 1)),
        ))
    res = run_bass_kernel_spmd(nc, in_maps, core_ids=list(range(8)))
    if _return_raw:
        return res
    outp = np.concatenate([np.asarray(res.results[c]["out"]) for c in range(8)], axis=0)
    return outp
